# version 14
# baseline (speedup 1.0000x reference)
import contextlib
import numpy as np
import concourse.bass as bass
import concourse.mybir as mybir
from concourse.bass_utils import run_bass_kernel_spmd

F32 = mybir.dt.float32
BF16 = mybir.dt.bfloat16
AF = mybir.ActivationFunctionType
ALU = mybir.AluOpType
AX = mybir.AxisListType

T_CORE = 4096
D = 1024
I_FFN = 2816
NT = 32
NG = 8
NPAIR = 8
EPS_LN = 1e-5
IO_CHUNKS = [(0, 512), (512, 512), (1024, 512), (1536, 512), (2048, 512), (2560, 256)]
NIO = 6
NC22 = 22

SC_P1 = 3 * NT
VE_P2_0 = 34
GP_P2_0 = 17
GP_G = 18

_MM_SIZES = {"q": 8, "nd": 8, "o": 4, "gu": 24, "down": 22}
_MM_ORDER = [("q", 0), ("nd", 0)]
for _g in range(NG):
    _MM_ORDER += [("o", _g)]
    if _g + 1 < NG:
        _MM_ORDER += [("q", _g + 1), ("nd", _g + 1)]
    _MM_ORDER += [("gu", _g), ("down", _g)]
MMs = {}
_c = 0
for _k in _MM_ORDER:
    MMs[_k] = _c
    _c += _MM_SIZES[_k[0]]
assert _c == NG * 66

_SC_SIZES = {"qer": 16, "nd": 16, "outs": 4, "xf": 4, "silu": 24}
_SC_ORDER = [("qer", 0), ("nd", 0), ("xf", 0)]
for _g in range(NG):
    if _g + 1 < NG:
        _SC_ORDER += [("qer", _g + 1), ("nd", _g + 1)]
    _SC_ORDER += [("silu", _g), ("outs", _g)]
    if _g + 1 < NG:
        _SC_ORDER += [("xf", _g + 1)]
SCs = {}
_c = SC_P1
for _k in _SC_ORDER:
    SCs[_k] = _c
    _c += _SC_SIZES[_k[0]]
SC_END = _c

_VE_SIZES = {"comb": 8, "div": 8, "y0": 4, "ln1": 2, "xq": 4, "ext": 36,
             "dev": 4, "ln2": 2}
_VE_ORDER = [("comb", 0), ("div", 0)]
for _g in range(NG):
    _VE_ORDER += [("y0", _g), ("ln1", _g)]
    if _g + 1 < NG:
        _VE_ORDER += [("comb", _g + 1)]
    _VE_ORDER += [("xq", _g), ("ext", _g)]
    if _g + 1 < NG:
        _VE_ORDER += [("div", _g + 1)]
    _VE_ORDER += [("dev", _g), ("ln2", _g)]
VEs = {}
_c = VE_P2_0
for _k in _VE_ORDER:
    VEs[_k] = _c
    _c += _VE_SIZES[_k[0]]
VE_END = _c

def EXTi(g, c, t):
    return VEs[("ext", g)] + 6 * c + (t + 1 if t < 2 else t + 2)
def CHi(g, c, j):
    return VEs[("ext", g)] + 6 * c + (3 if j == 0 else 6)

GM_PRO = 1
def GM_SHLD(k): return 2 + k
def GM_XQ(g): return 10 + 7 * g
def GM_HQC(g, c): return 11 + 7 * g + c

PT_SPECS = ([("g", c) for c in range(8)] + [("u", c) for c in range(8)]
            + [("d", c) for c in range(NC22)])
PT_CC = {"g": 1, "u": 2, "d": 3}
NPT = len(PT_SPECS)

SHARD_TILES = [
    ("g", 0, 128, D), ("g", 128, 128, D), ("g", 256, 96, D),
    ("u", 0, 128, D), ("u", 128, 128, D), ("u", 256, 96, D),
    ("d", 0, 128, 1408), ("d", 1408, 128, 1408),
]
SHARD_CHAINS = [
    (0, 128, 0, D, ("g", 0)), (1, 128, 0, D, ("g", 128)), (2, 96, 0, D, ("g", 256)),
    (3, 128, 0, D, ("u", 0)), (4, 128, 0, D, ("u", 128)), (5, 96, 0, D, ("u", 256)),
    (6, 128, 0, 1024, None), (6, 128, 1024, 384, ("dcol", 0)),
    (7, 128, 0, 1024, None), (7, 128, 1024, 384, ("dcol", 1408)),
]
NCH = 10


def emit_quant_dve(vector, x_ap, nblk, F, qs, vsync):
    P = x_ap.shape[0]
    bm = qs["bm"][0:P, 0:nblk]
    sc = qs["sc"][0:P, 0:nblk]
    rcp = qs["rcp"][0:P, 0:nblk]
    y = qs["y"][0:P, 0:F]
    c2 = qs["c2"][0:P, 0:F]
    w4 = qs["w4"][0:P, 0:F]
    t16 = qs["t16"][0:P, 0:F]
    r4 = qs["r4"][0:P, 0:F]
    d2 = qs["d2"][0:P, 0:F]
    vsync(vector.tensor_reduce(bm, x_ap.rearrange("p (b k) -> p b k", k=16), AX.X,
                               ALU.max, apply_absolute_value=True))
    vsync(vector.tensor_scalar(sc, bm, 1.0 / 6.0, 1e-12, ALU.mult, ALU.max))
    vsync(vector.reciprocal(rcp, sc))
    vsync(vector.tensor_scalar_mul(sc, sc, 0.25))
    rcp_rep = rcp.unsqueeze(2).broadcast_to([P, nblk, 16])
    vector.tensor_tensor(y.rearrange("p (b k) -> p b k", k=16),
                         x_ap.rearrange("p (b k) -> p b k", k=16), rcp_rep, ALU.mult)
    vector.tensor_scalar(c2, y, 2.0, -2.0, ALU.min, ALU.max)
    vector.scalar_tensor_tensor(w4, c2, 2.0, y, ALU.mult, ALU.add)
    vector.tensor_scalar(c2, y, 4.0, -4.0, ALU.min, ALU.max)
    vector.tensor_tensor(w4, w4, c2, ALU.add)
    vector.tensor_scalar_mul(t16, w4, 0.5)
    vector.tensor_scalar_mul(r4, t16, 2.0)
    vector.tensor_scalar(d2, r4, 8.0, -8.0, ALU.min, ALU.max)
    vector.scalar_tensor_tensor(d2, r4, 4.0, d2, ALU.mult, ALU.subtract)
    vector.tensor_scalar(r4, r4, 12.0, -12.0, ALU.min, ALU.max)
    vector.scalar_tensor_tensor(r4, r4, -2.0, d2, ALU.mult, ALU.add)


def emit_quant_gp_ret(gpsimd, q_out, nblk, F, qs):
    Pn = q_out.shape[0]
    sc_rep = qs["sc"][0:Pn, 0:nblk].unsqueeze(2).broadcast_to([Pn, nblk, 16])
    return gpsimd.tensor_tensor(q_out.rearrange("p (b k) -> p b k", k=16),
                                qs["r4"][0:Pn, 0:F].rearrange("p (b k) -> p b k", k=16),
                                sc_rep, ALU.mult)


def build_nc():
    nc = bass.Bass("TRN2", target_bir_lowering=False, debug=False,
                   detect_race_conditions=False)
    x_d = nc.dram_tensor("x", [T_CORE, D], F32, kind="ExternalInput").ap()
    w_in = {k: nc.dram_tensor("W" + k, [D, D], F32, kind="ExternalInput").ap()
            for k in ("q", "k", "v", "o")}
    bq_d = nc.dram_tensor("bq", [D], F32, kind="ExternalInput").ap()
    bk_d = nc.dram_tensor("bk", [D], F32, kind="ExternalInput").ap()
    bv_d = nc.dram_tensor("bv", [D], F32, kind="ExternalInput").ap()
    bo_d = nc.dram_tensor("bo", [D], F32, kind="ExternalInput").ap()
    wg_sh_d = nc.dram_tensor("wg_sh", [I_FFN // 8, D], F32, kind="ExternalInput").ap()
    wu_sh_d = nc.dram_tensor("wu_sh", [I_FFN // 8, D], F32, kind="ExternalInput").ap()
    wd_sh_d = nc.dram_tensor("wd_sh", [D // 8, I_FFN], F32, kind="ExternalInput").ap()
    out_d = nc.dram_tensor("out", [T_CORE, D], F32, kind="ExternalOutput").ap()

    xbf_d = nc.dram_tensor("xbf_d", [T_CORE, D], BF16, kind="Internal").ap()
    wbf = {k: nc.dram_tensor("wbf_" + k, [D, D], BF16, kind="Internal").ap()
           for k in ("q", "k", "v", "o")}
    qin = {m: nc.dram_tensor("q%s_in" % m, [I_FFN // 8 * D if m != "d" else D // 8 * I_FFN],
                             BF16, kind="Internal").ap() for m in ("g", "u", "d")}
    qout = {m: nc.dram_tensor("q%s_out" % m, [I_FFN * D], BF16, kind="Internal").ap()
            for m in ("g", "u", "d")}
    wgT_d = nc.dram_tensor("wgT_d", [D, I_FFN], BF16, kind="Internal").ap()
    wuT_d = nc.dram_tensor("wuT_d", [D, I_FFN], BF16, kind="Internal").ap()
    wdT_d = nc.dram_tensor("wdT_d", [I_FFN, D], BF16, kind="Internal").ap()
    kv_in = nc.dram_tensor("kv_in", [128 * 1032], F32, kind="Internal").ap()
    kv_out = nc.dram_tensor("kv_out", [128 * 1032], F32, kind="Internal").ap()
    xq_d = nc.dram_tensor("xq_d", [512, D], BF16, kind="Internal").ap()
    hq_d = nc.dram_tensor("hq_d", [512, I_FFN], BF16, kind="Internal").ap()

    LM = {}
    L_PT = []

    with contextlib.ExitStack() as stack:
        E = stack.enter_context
        wA = E(nc.sbuf_tensor("wA", [128, 8 * D], BF16))
        wB = E(nc.sbuf_tensor("wB", [128, 8 * D], BF16))
        xT0 = E(nc.sbuf_tensor("xT0", [128, 8 * 512], BF16))
        xT1 = E(nc.sbuf_tensor("xT1", [128, 8 * 512], BF16))
        k_sb = E(nc.sbuf_tensor("k_sb", [128, D], BF16))
        v_sb = E(nc.sbuf_tensor("v_sb", [128, D], BF16))
        e_sb = E(nc.sbuf_tensor("e_sb", [128, D], F32))
        r_sb = E(nc.sbuf_tensor("r_sb", [128, D], F32))
        kvs = E(nc.sbuf_tensor("kvs", [128, 1032], F32))
        kvbd = E(nc.sbuf_tensor("kvbd", [128, NPAIR * 128], BF16))
        ksr = E(nc.sbuf_tensor("ksr", [128, NPAIR * 128], BF16))
        q_elu = E(nc.sbuf_tensor("q_elu", [128, 8 * 512], BF16))
        xf = E(nc.sbuf_tensor("xf", [128, 4 * D], F32))
        y1b = E(nc.sbuf_tensor("y1b", [128, 4 * D], F32))
        y0s = E(nc.sbuf_tensor("y0s", [128, D], F32))
        sqs = E(nc.sbuf_tensor("sqs", [128, D], F32))
        hq_sb = E(nc.sbuf_tensor("hq_sb", [128, 4 * I_FFN], BF16))
        wgu = E(nc.sbuf_tensor("wgu", [128, 2 * 2 * 8 * 512], BF16))
        wdt = E(nc.sbuf_tensor("wdt", [128, 2 * D], BF16))
        qs = {
            "bm": E(nc.sbuf_tensor("q_bm", [128, 64], F32)),
            "sc": E(nc.sbuf_tensor("q_sc", [128, 64], F32)),
            "rcp": E(nc.sbuf_tensor("q_rcp", [128, 64], F32)),
            "y": E(nc.sbuf_tensor("q_y", [128, 1024], F32)),
            "c2": E(nc.sbuf_tensor("q_c2", [128, 1024], F32)),
            "w4": E(nc.sbuf_tensor("q_w4", [128, 1024], F32)),
            "t16": E(nc.sbuf_tensor("q_t16", [128, 1024], mybir.dt.int16)),
            "r4": E(nc.sbuf_tensor("q_r4", [128, 1024], BF16)),
            "d2": E(nc.sbuf_tensor("q_d2", [128, 1024], BF16)),
        }
        shw = E(nc.sbuf_tensor("shw", [128, 1408], F32))
        shq = E(nc.sbuf_tensor("shq", [128, 1408], BF16))
        lnc = E(nc.sbuf_tensor("lnc", [128, 32], F32))
        ones1 = E(nc.sbuf_tensor("ones1", [1, 128], BF16))
        onecol = E(nc.sbuf_tensor("onecol", [128, 1], BF16))
        mhalf = E(nc.sbuf_tensor("mhalf", [128, 1], F32))
        brows = E(nc.sbuf_tensor("brows", [1, 3 * D], BF16))
        bq_sb = E(nc.sbuf_tensor("bq_sb", [128, 8], F32))
        P = [E(nc.psum_tensor("P%d" % i, [128, 1024], F32)) for i in range(4)]

        e2 = kvs[:, 0:1024]
        r2 = shw[:, 0:1024]

        s_ld = E(nc.semaphore("s_ld"))
        s_gpd = E(nc.semaphore("s_gpd"))
        s_gpc = E(nc.semaphore("s_gpc"))
        s_cc = E(nc.semaphore("s_cc"))
        s_mmp = E(nc.semaphore("s_mmp"))
        s_mmk = E(nc.semaphore("s_mmk"))
        s_mm = E(nc.semaphore("s_mm"))
        s_ve = E(nc.semaphore("s_ve"))
        s_vq = E(nc.semaphore("s_vq"))
        s_sc = E(nc.semaphore("s_sc"))
        s_st = E(nc.semaphore("s_st"))
        s_gm = E(nc.semaphore("s_gm"))
        s_lm = E(nc.semaphore("s_lm"))
        s_vs = E(nc.semaphore("s_vs"))
        s_pt = E(nc.semaphore("s_pt"))
        s_kvd = E(nc.semaphore("s_kvd"))

        block = E(nc.Block())

        @block.sync
        def _(sync):
            ld = 0
            lm = 0
            def dma(out, in_, transpose=False, slow=False):
                nonlocal ld
                sync.dma_start(out=out, in_=in_, transpose=transpose,
                               allow_slow_non_contiguous=slow).then_inc(s_ld, 16)
                ld += 16
            def milestone(key):
                nonlocal lm
                sync.wait_ge(s_ld, ld)
                sync.sem_inc(s_lm, 1)
                lm += 1
                LM[key] = lm
            def load_xT0(g):
                for c in range(8):
                    dma(xT0[:, c * 512:(c + 1) * 512],
                        xbf_d[g * 512:(g + 1) * 512, c * 128:(c + 1) * 128],
                        transpose=True)
                milestone(("xT0", g))
            def load_xf(g):
                for t in range(4):
                    dma(xf[:, t * D:(t + 1) * D],
                        x_d[g * 512 + t * 128: g * 512 + (t + 1) * 128, :])
                milestone(("xf", g))
            dma(bq_sb[:], bq_d.rearrange("(c p) -> p c", p=128), slow=True)
            sync.wait_ge(s_gm, GM_PRO)
            for name, dst in (("k", wA), ("v", wB)):
                for c in range(8):
                    dma(dst[:, c * D:(c + 1) * D],
                        wbf[name][:, c * 128:(c + 1) * 128], transpose=True)
            milestone("wkv")
            for g in range(NG):
                if g >= 2:
                    sync.wait_ge(s_mmp, 4 * (g - 1))
                buf = xT0 if g % 2 == 0 else xT1
                for c in range(8):
                    dma(buf[:, c * 512:(c + 1) * 512],
                        xbf_d[g * 512:(g + 1) * 512, c * 128:(c + 1) * 128],
                        transpose=True)
                milestone(("p1x", g))
            for j, (m, c) in enumerate(PT_SPECS):
                sync.wait_ge(s_cc, PT_CC[m])
                if j >= 4:
                    sync.wait_ge(s_pt, j - 3)
                rows = I_FFN if m in ("g", "u") else D
                cols = D if m in ("g", "u") else I_FFN
                srcm = qout[m][0:I_FFN * D].rearrange("(r c) -> r c", c=cols)
                dma(hq_sb[:, (j % 4) * I_FFN:(j % 4) * I_FFN + rows],
                    srcm[:, c * 128:(c + 1) * 128], transpose=True)
                L_PT.append(ld)
            sync.wait_ge(s_mmk, NT)
            for name, dst in (("q", wA), ("o", wB)):
                for c in range(8):
                    dma(dst[:, c * D:(c + 1) * D],
                        wbf[name][:, c * 128:(c + 1) * 128], transpose=True)
            milestone("wqo")
            load_xT0(0)
            load_xf(0)
            sync.wait_ge(s_cc, 4)
            dma(kvs[:], kv_out.rearrange("(p n) -> p n", p=128))
            milestone("kvrel")
            for g in range(NG):
                for c in range(2):
                    if g == 0:
                        sync.wait_ge(s_pt, 16)
                    else:
                        sync.wait_ge(s_mm, MMs[("gu", g - 1)] + (20 if c == 0 else 24))
                    io0, iow = IO_CHUNKS[c]
                    h0 = (c % 2) * 8192
                    dma(wgu[:, h0:h0 + 4096].rearrange("p (a w) -> p a w", w=512)[:, :, 0:iow],
                        wgT_d.rearrange("(a p) io -> p a io", p=128)[:, :, io0:io0 + iow])
                    dma(wgu[:, h0 + 4096:h0 + 8192].rearrange("p (a w) -> p a w", w=512)[:, :, 0:iow],
                        wuT_d.rearrange("(a p) io -> p a io", p=128)[:, :, io0:io0 + iow])
                    milestone(("guc", g, c))
                if g + 1 < NG:
                    sync.wait_ge(s_ve, VEs[("div", g)] + 8)
                    load_xT0(g + 1)
                for c in range(2):
                    if g == 0:
                        sync.wait_ge(s_pt, 17 + c)
                    else:
                        sync.wait_ge(s_mm, MMs[("down", g - 1)] + 21 + c)
                    dma(wdt[:, (c % 2) * D:(c % 2 + 1) * D], wdT_d[c * 128:(c + 1) * 128, :])
                    milestone(("dn", g, c))
                if g >= 1:
                    sync.wait_ge(s_ve, VEs[("ln2", g - 1)] + 2)
                    load_xf(g)
                sync.wait_ge(s_gm, GM_XQ(g))
                for c in range(8):
                    dma(xT1[:, c * 512:(c + 1) * 512],
                        xq_d[:, c * 128:(c + 1) * 128], transpose=True)
                milestone(("xqT", g))
                for c in range(2, NIO):
                    io0, iow = IO_CHUNKS[c]
                    h0 = (c % 2) * 8192
                    sync.wait_ge(s_mm, MMs[("gu", g)] + 4 * (c - 1))
                    dma(wgu[:, h0:h0 + 4096].rearrange("p (a w) -> p a w", w=512)[:, :, 0:iow],
                        wgT_d.rearrange("(a p) io -> p a io", p=128)[:, :, io0:io0 + iow])
                    dma(wgu[:, h0 + 4096:h0 + 8192].rearrange("p (a w) -> p a w", w=512)[:, :, 0:iow],
                        wuT_d.rearrange("(a p) io -> p a io", p=128)[:, :, io0:io0 + iow])
                    milestone(("guc", g, c))
                nco = 0
                for c in range(NIO):
                    io0, iow = IO_CHUNKS[c]
                    sync.wait_ge(s_gm, GM_HQC(g, c))
                    for cc in range(io0 // 128, (io0 + iow) // 128):
                        dma(hq_sb[:, cc * 512:(cc + 1) * 512],
                            hq_d[:, cc * 128:(cc + 1) * 128], transpose=True)
                        nco += 1
                assert nco == NC22
                milestone(("hqT", g))
                for c in range(2, NC22):
                    sync.wait_ge(s_mm, MMs[("down", g)] + (c - 2) + 1)
                    dma(wdt[:, (c % 2) * D:(c % 2 + 1) * D], wdT_d[c * 128:(c + 1) * 128, :])
                    milestone(("dnc", g, c))

        @block.gpsimd
        def _(gpsimd):
            gpd = 0
            gpc = 0
            gm = 0
            def gdma(out, in_):
                nonlocal gpd
                gpsimd.dma_start(out=out, in_=in_).then_inc(s_gpd, 16)
                gpd += 16
            def ginc(inst):
                nonlocal gpc
                inst.then_inc(s_gpc, 1)
                gpc += 1
            def gmilestone(expect=None):
                nonlocal gm
                gpsimd.wait_ge(s_gpd, gpd)
                gpsimd.sem_inc(s_gm, 1)
                gm += 1
                if expect is not None:
                    assert gm == expect, (gm, expect)
            ginc(gpsimd.memset(ones1[:], 1.0))
            ginc(gpsimd.memset(onecol[:], 1.0))
            ginc(gpsimd.memset(mhalf[:], -0.5))
            ginc(gpsimd.memset(kvbd[:], 0.0))
            ginc(gpsimd.memset(ksr[:], 0.0))
            ginc(gpsimd.engine_nop())
            ginc(gpsimd.engine_nop())
            assert gpc == 7
            for j in range(4):
                gdma(xbf_d[j * 1024:(j + 1) * 1024, :], x_d[j * 1024:(j + 1) * 1024, :])
            for name in ("q", "k", "v", "o"):
                gdma(wbf[name], w_in[name])
            for j, b in enumerate((bk_d, bv_d, bo_d)):
                gdma(brows[:, j * D:(j + 1) * D], b[None, :])
            gmilestone(GM_PRO)
            def shard_load(k):
                m, r0, nr, w = SHARD_TILES[k]
                if m == "d":
                    gdma(shw[0:128, 0:w], wd_sh_d[:, r0:r0 + w])
                else:
                    srct = {"g": wg_sh_d, "u": wu_sh_d}[m]
                    gdma(shw[0:nr, 0:w], srct[r0:r0 + nr, :])
            shard_load(0)
            gmilestone(GM_SHLD(0))
            for j, (li, nr, c0, w, wr) in enumerate(SHARD_CHAINS):
                gpsimd.wait_ge(s_vq, j + 1)
                if j + 1 < len(SHARD_CHAINS) and SHARD_CHAINS[j + 1][0] == li + 1:
                    shard_load(li + 1)
                    gmilestone(GM_SHLD(li + 1))
                gpsimd.wait_ge(s_gpd, gpd)
                ginc(emit_quant_gp_ret(gpsimd, shq[0:nr, c0:c0 + w], w // 16, w, qs))
                if wr is not None:
                    kind, base = wr
                    if kind == "dcol":
                        gdma(qin["d"][0:128 * I_FFN].rearrange("(r c) -> r c", c=I_FFN)[:, base:base + 1408],
                             shq[0:128, 0:1408])
                    else:
                        gdma(qin[kind][base * D:(base + nr) * D].rearrange("(r c) -> r c", c=D),
                             shq[0:nr, 0:D])
            gpsimd.wait_ge(s_gpd, gpd)
            assert gpc == GP_P2_0
            gpsimd.collective_compute("AllGather", ALU.bypass,
                                      replica_groups=[[0, 1, 2, 3, 4, 5, 6, 7]],
                                      ins=[qin["g"]], outs=[qout["g"]]).then_inc(s_cc, 1)
            gpsimd.collective_compute("AllGather", ALU.bypass,
                                      replica_groups=[[0, 1, 2, 3, 4, 5, 6, 7]],
                                      ins=[qin["u"]], outs=[qout["u"]]).then_inc(s_cc, 1)
            gpsimd.collective_compute("AllGather", ALU.bypass,
                                      replica_groups=[[0, 1, 2, 3, 4, 5, 6, 7]],
                                      ins=[qin["d"]], outs=[qout["d"]]).then_inc(s_cc, 1)
            gpd_after = []
            for j, (m, c) in enumerate(PT_SPECS):
                gpsimd.wait_ge(s_ld, L_PT[j])
                rows = I_FFN if m in ("g", "u") else D
                dstT = {"g": wgT_d, "u": wuT_d, "d": wdT_d}[m]
                gdma(dstT[c * 128:(c + 1) * 128, :],
                     hq_sb[:, (j % 4) * I_FFN:(j % 4) * I_FFN + rows])
                gpd_after.append(gpd)
                if j >= 1:
                    gpsimd.wait_ge(s_gpd, gpd_after[j - 1])
                    gpsimd.sem_inc(s_pt, 1)
            gpsimd.wait_ge(s_gpd, gpd_after[-1])
            gpsimd.sem_inc(s_pt, 1)
            gpsimd.wait_ge(s_kvd, 16)
            gpsimd.collective_compute("AllReduce", ALU.add,
                                      replica_groups=[[0, 1, 2, 3], [4, 5, 6, 7]],
                                      ins=[kv_in], outs=[kv_out]).then_inc(s_cc, 1)
            for g in range(NG):
                base_gp = GP_P2_0 + g * GP_G
                gpsimd.wait_ge(s_ve, VEs[("ln1", g)] + 1)
                ginc(gpsimd.tensor_tensor(lnc[:, 12:16], lnc[:, 4:8],
                                          mhalf[:].broadcast_to([128, 4]), ALU.pow))
                assert gpc == base_gp + 1
                for t in range(4):
                    gpsimd.wait_ge(s_ve, VEs[("xq", g)] + t + 1)
                    ginc(emit_quant_gp_ret(gpsimd, xT1[:, t * D:(t + 1) * D], 64, D, qs))
                assert gpc == base_gp + 5
                gdma(xq_d.rearrange("(t p) c -> p t c", p=128),
                     xT1[:].rearrange("p (t c) -> p t c", c=D))
                gmilestone(GM_XQ(g))
                for c in range(NIO):
                    io0, iow = IO_CHUNKS[c]
                    for j in range(2):
                        gpsimd.wait_ge(s_ve, CHi(g, c, j))
                        blk = 4 * io0 + 2 * j * iow
                        ginc(emit_quant_gp_ret(
                            gpsimd, hq_sb[:, blk:blk + 2 * iow], 2 * iow // 16,
                            2 * iow, qs))
                    gdma(hq_d.rearrange("(t p) z -> p t z", p=128)[:, :, io0:io0 + iow],
                         hq_sb[:, 4 * io0:4 * (io0 + iow)].rearrange(
                             "p (t w) -> p t w", w=iow))
                    gmilestone(GM_HQC(g, c))
                gpsimd.wait_ge(s_ve, VEs[("ln2", g)] + 1)
                ginc(gpsimd.tensor_tensor(lnc[:, 28:32], lnc[:, 20:24],
                                          mhalf[:].broadcast_to([128, 4]), ALU.pow))
                assert gpc == base_gp + GP_G

        @block.tensor
        def _(tensor):
            mmp = mmk = 0
            mm = [0]
            def minc(inst):
                inst.then_inc(s_mm, 1)
                mm[0] += 1
            for i in range(NT + 1):
                if i < NT:
                    g, tl = i // 4, i % 4
                    if i == 0:
                        tensor.wait_ge(s_gpc, 7)
                        tensor.wait_ge(s_gm, GM_PRO)
                        tensor.wait_ge(s_lm, LM["wkv"])
                    if tl == 0:
                        tensor.wait_ge(s_lm, LM[("p1x", g)])
                    tensor.wait_ge(s_sc, 3 * i)
                    xbuf = xT0 if g % 2 == 0 else xT1
                    for dst, wT, bias_off in ((P[0], wA, 0), (P[1], wB, D)):
                        for h in range(2):
                            for cf in range(8):
                                tensor.matmul(
                                    dst[:, h * 512:(h + 1) * 512],
                                    xbuf[:, cf * 512 + tl * 128: cf * 512 + (tl + 1) * 128],
                                    wT[:, cf * D + h * 512: cf * D + (h + 1) * 512],
                                    start=(cf == 0), stop=False)
                            last = tensor.matmul(dst[:, h * 512:(h + 1) * 512], ones1[:],
                                          brows[:, bias_off + h * 512: bias_off + (h + 1) * 512],
                                          start=False, stop=True)
                    last.then_inc(s_mmp, 1)
                    mmp += 1
                if i >= 1:
                    j = i - 1
                    tensor.wait_ge(s_ve, j + 1)
                    if i == NT:
                        tensor.wait_ge(s_sc, 3 * NT)
                    for p in range(NPAIR):
                        tensor.matmul(P[2][:, p * 128:(p + 1) * 128],
                                      k_sb[:, p * 128:(p + 1) * 128],
                                      v_sb[:, p * 128:(p + 1) * 128],
                                      start=(j == 0 and p % 4 == 0),
                                      stop=(j == NT - 1 and p % 4 == 3))
                        last = tensor.matmul(P[3][:, p:p + 1],
                                      k_sb[:, p * 128:(p + 1) * 128],
                                      onecol[:],
                                      start=(j == 0 and p == 0),
                                      stop=(j == NT - 1 and p == 7))
                    last.then_inc(s_mmk, 1)
                    mmk += 1
            assert mmp == NT and mmk == NT

            def emit_q(gq):
                assert mm[0] == MMs[("q", gq)]
                tensor.wait_ge(s_lm, LM[("xT0", gq)])
                for co in range(8):
                    if co == 0:
                        if gq == 0:
                            tensor.wait_ge(s_ve, 33)
                        else:
                            tensor.wait_ge(s_ve, VEs[("y0", gq - 1)] + 4)
                    if co >= 4:
                        tensor.wait_ge(s_sc, SCs[("qer", gq)] + 2 * (co - 3))
                    pdst = P[2 + (co % 4) // 2]
                    for cf in range(8):
                        last = tensor.matmul(pdst[:, (co % 2) * 512:(co % 2) * 512 + 512],
                                      wA[:, cf * D + co * 128: cf * D + (co + 1) * 128],
                                      xT0[:, cf * 512:(cf + 1) * 512],
                                      start=(cf == 0), stop=(cf == 7))
                    minc(last)

            def emit_nd(gq):
                assert mm[0] == MMs[("nd", gq)]
                for p in range(NPAIR):
                    tensor.wait_ge(s_ve, VEs[("comb", gq)] + p + 1)
                    if p == 0:
                        tensor.wait_ge(s_sc, SCs[("qer", gq)] + 14)
                    elif p == 1:
                        tensor.wait_ge(s_sc, SCs[("qer", gq)] + 16)
                    else:
                        tensor.wait_ge(s_sc, SCs[("nd", gq)] + 2 * (p - 1))
                    half = (p % 2) * 512
                    tensor.matmul(P[2][:, half:half + 512],
                                  kvbd[:, p * 128:(p + 1) * 128],
                                  q_elu[:, p * 512:(p + 1) * 512],
                                  start=True, stop=True)
                    last = tensor.matmul(P[3][:, half:half + 512],
                                  ksr[:, p * 128:(p + 1) * 128],
                                  q_elu[:, p * 512:(p + 1) * 512],
                                  start=True, stop=True)
                    minc(last)

            emit_q(0)
            emit_nd(0)
            for g in range(NG):
                assert mm[0] == MMs[("o", g)]
                for t in range(4):
                    if t == 0:
                        if g == 0:
                            tensor.wait_ge(s_ve, VEs[("div", 0)] + 8)
                        else:
                            tensor.wait_ge(s_ve, VEs[("dev", g - 1)] + 3)
                    else:
                        tensor.wait_ge(s_ve, VEs[("y0", g)] + t)
                    for h in range(2):
                        for p in range(NPAIR):
                            tensor.matmul(P[2][:, h * 512:(h + 1) * 512],
                                          q_elu[:, p * 512 + t * 128: p * 512 + (t + 1) * 128],
                                          wB[:, p * D + h * 512: p * D + (h + 1) * 512],
                                          start=(p == 0), stop=False)
                        last = tensor.matmul(P[2][:, h * 512:(h + 1) * 512], ones1[:],
                                      brows[:, 2 * D + h * 512: 2 * D + (h + 1) * 512],
                                      start=False, stop=True)
                    minc(last)
                if g + 1 < NG:
                    emit_q(g + 1)
                    emit_nd(g + 1)
                assert mm[0] == MMs[("gu", g)]
                tensor.wait_ge(s_lm, LM[("xqT", g)])
                if g + 1 < NG:
                    tensor.wait_ge(s_sc, SCs[("nd", g + 1)] + 16)
                for idx in range(NIO * 4):
                    c, t = idx // 4, idx % 4
                    io0, iow = IO_CHUNKS[c]
                    h0 = (c % 2) * 8192
                    if t == 0:
                        tensor.wait_ge(s_lm, LM[("guc", g, c)])
                    if t >= 2:
                        tensor.wait_ge(s_ve, EXTi(g, c, t - 2))
                    elif c >= 1:
                        tensor.wait_ge(s_ve, EXTi(g, c - 1, t + 2))
                    pt = P[2 + idx % 2]
                    for cf in range(8):
                        tensor.matmul(pt[:, 0:iow],
                                      xT1[:, cf * 512 + t * 128: cf * 512 + (t + 1) * 128],
                                      wgu[:, h0 + cf * 512: h0 + cf * 512 + iow],
                                      start=(cf == 0), stop=(cf == 7))
                    for cf in range(8):
                        last = tensor.matmul(pt[:, 512:512 + iow],
                                      xT1[:, cf * 512 + t * 128: cf * 512 + (t + 1) * 128],
                                      wgu[:, h0 + 4096 + cf * 512: h0 + 4096 + cf * 512 + iow],
                                      start=(cf == 0), stop=(cf == 7))
                    minc(last)
                assert mm[0] == MMs[("down", g)]
                tensor.wait_ge(s_ve, VEs[("ext", g)] + 36)
                tensor.wait_ge(s_lm, LM[("hqT", g)])
                for c in range(NC22):
                    tensor.wait_ge(s_lm, LM[("dn", g, c) if c < 2 else ("dnc", g, c)])
                    for t in range(4):
                        for h in range(2):
                            last = tensor.matmul(P[t][:, h * 512:(h + 1) * 512],
                                          hq_sb[:, c * 512 + t * 128: c * 512 + (t + 1) * 128],
                                          wdt[:, (c % 2) * D + h * 512: (c % 2) * D + (h + 1) * 512],
                                          start=(c == 0), stop=(c == NC22 - 1))
                    minc(last)
            assert mm[0] == NG * 66

        @block.vector
        def _(vector):
            ve = [0]
            vq = 0
            vs_n = [0]
            def vinc(inst):
                inst.then_inc(s_ve, 1)
                ve[0] += 1
            def vsync(inst):
                vs_n[0] += 1
                inst.then_inc(s_vs, 1)
                vector.wait_ge(s_vs, vs_n[0])
            for i in range(NT):
                vector.wait_ge(s_mmk, i)
                vector.wait_ge(s_sc, 3 * i + 2)
                vinc(vector.scalar_tensor_tensor(k_sb[:], e_sb[:], 1.0, r_sb[:],
                                                 ALU.min, ALU.add))
                assert ve[0] == i + 1
                if i < NCH:
                    li, nr, c0, w, _wr = SHARD_CHAINS[i]
                    vector.wait_ge(s_gm, GM_SHLD(li))
                    vector.wait_ge(s_gpc, 7 + i)
                    emit_quant_dve(vector, shw[0:nr, c0:c0 + w], w // 16, w, qs, vsync)
                    vector.engine_nop().then_inc(s_vq, 1)
                    vq += 1
            vector.wait_ge(s_mmk, NT)
            vector.tensor_copy(kvs[:, 0:512], P[2][:, 0:512])
            vector.tensor_copy(kvs[:, 512:1024], P[2][:, 512:1024])
            vinc(vector.tensor_copy(kvs[:, 1024:1032], P[3][:, 0:8]))
            assert ve[0] == 33
            vector.wait_ge(s_lm, LM["kvrel"])
            vector.wait_ge(s_gpc, 7)
            for p in range(NPAIR):
                src0 = kvs[0:64, p * 128: p * 128 + 64]
                src1 = kvs[64:128, p * 128 + 64: p * 128 + 128]
                vector.tensor_copy(kvbd[0:64, p * 128: p * 128 + 64], src0)
                vector.tensor_copy(kvbd[64:128, p * 128 + 64: p * 128 + 128], src1)
                ks = kvs[:, 1024 + p:1025 + p]
                vector.tensor_copy(ksr[0:64, p * 128: p * 128 + 64],
                                   ks[0:64, :].broadcast_to([64, 64]))
                vector.tensor_copy(ksr[64:128, p * 128 + 64: p * 128 + 128],
                                   ks[64:128, :].broadcast_to([64, 64]))
            vinc(vector.engine_nop())
            assert ve[0] == VE_P2_0

            def emit_comb(gq):
                assert ve[0] == VEs[("comb", gq)]
                eb = e_sb if gq == 0 else e2
                rb = r_sb if gq == 0 else r2
                for co in range(8):
                    vector.wait_ge(s_sc, SCs[("qer", gq)] + 2 * (co + 1))
                    half = (co % 2) * 512
                    vinc(vector.scalar_tensor_tensor(
                        q_elu[:, co * 512:(co + 1) * 512],
                        eb[:, half:half + 512], 1.0, rb[:, half:half + 512],
                        ALU.min, ALU.add))

            def emit_div(gq):
                assert ve[0] == VEs[("div", gq)]
                for p in range(NPAIR):
                    vector.wait_ge(s_sc, SCs[("nd", gq)] + 2 * (p + 1))
                    vector.tensor_scalar_max(sqs[:, 0:512],
                                             q_elu[:, p * 512:(p + 1) * 512], 1e-6)
                    vector.reciprocal(sqs[:, 512:1024], sqs[:, 0:512])
                    vinc(vector.tensor_tensor(q_elu[:, p * 512:(p + 1) * 512],
                                              xT0[:, p * 512:(p + 1) * 512],
                                              sqs[:, 512:1024], ALU.mult))

            emit_comb(0)
            emit_div(0)
            for g in range(NG):
                assert ve[0] == VEs[("y0", g)]
                vector.wait_ge(s_lm, LM[("xf", g)])
                for t in range(4):
                    vector.wait_ge(s_mm, MMs[("o", g)] + t + 1)
                    if g >= 1:
                        vector.wait_ge(s_sc, SCs[("outs", g - 1)] + t + 1)
                        vector.wait_ge(s_st, 16 * (4 * (g - 1) + t + 1))
                    vinc(vector.scalar_tensor_tensor(
                        y1b[:, t * D:(t + 1) * D], P[2][:], 0.0, xf[:, t * D:(t + 1) * D],
                        ALU.add, ALU.add, accum_out=lnc[:, t:t + 1]))
                    vector.scalar_tensor_tensor(
                        sqs[:], y1b[:, t * D:(t + 1) * D], 0.0, y1b[:, t * D:(t + 1) * D],
                        ALU.add, ALU.mult, accum_out=lnc[:, 4 + t:5 + t])
                assert ve[0] == VEs[("ln1", g)]
                vsync(vector.tensor_scalar_mul(lnc[:, 8:12], lnc[:, 0:4], 1.0 / D))
                vsync(vector.tensor_tensor(lnc[:, 12:16], lnc[:, 8:12], lnc[:, 8:12], ALU.mult))
                vsync(vector.scalar_tensor_tensor(lnc[:, 4:8], lnc[:, 4:8], 1.0 / D,
                                                  lnc[:, 12:16], ALU.mult, ALU.subtract))
                vinc(vector.tensor_scalar_add(lnc[:, 4:8], lnc[:, 4:8], EPS_LN))
                vector.wait_ge(s_gpc, GP_P2_0 + g * GP_G + 1)
                vinc(vector.scalar_tensor_tensor(lnc[:, 8:12], lnc[:, 8:12], -1.0,
                                                 lnc[:, 12:16], ALU.mult, ALU.mult))
                if g + 1 < NG:
                    emit_comb(g + 1)
                assert ve[0] == VEs[("xq", g)]
                for t in range(4):
                    vector.wait_ge(s_sc, SCs[("xf", g)] + t + 1)
                    vector.wait_ge(s_gpc, GP_P2_0 + g * GP_G + 1 + t)
                    emit_quant_dve(vector, xf[:, t * D:(t + 1) * D], 64, D, qs, vsync)
                    vinc(vector.engine_nop())
                assert ve[0] == VEs[("ext", g)]
                for idx in range(NIO * 4):
                    c, t = idx // 4, idx % 4
                    io0, iow = IO_CHUNKS[c]
                    half = (idx % 2) * 512
                    pt = P[2 + idx % 2]
                    vector.wait_ge(s_sc, SCs[("silu", g)] + idx + 1)
                    vinc(vector.tensor_tensor(y0s[:, (t % 2) * iow:(t % 2 + 1) * iow],
                                              e_sb[:, half:half + iow],
                                              pt[:, 512:512 + iow], ALU.mult))
                    assert ve[0] == EXTi(g, c, t)
                    if t % 2 == 1:
                        j = t // 2
                        vector.wait_ge(s_gpc, GP_P2_0 + g * GP_G + 5 + 2 * c + j)
                        emit_quant_dve(vector, y0s[:, 0:2 * iow], 2 * iow // 16,
                                       2 * iow, qs, vsync)
                        vector.engine_nop().then_inc(s_ve, 1)
                        ve[0] += 1
                        assert ve[0] == CHi(g, c, j)
                if g + 1 < NG:
                    emit_div(g + 1)
                assert ve[0] == VEs[("dev", g)]
                for t in range(4):
                    vector.wait_ge(s_mm, MMs[("down", g)] + NC22)
                    vinc(vector.scalar_tensor_tensor(
                        y1b[:, t * D:(t + 1) * D], P[t][:], 0.0, xf[:, t * D:(t + 1) * D],
                        ALU.add, ALU.add, accum_out=lnc[:, 16 + t:17 + t]))
                    vector.scalar_tensor_tensor(
                        sqs[:], y1b[:, t * D:(t + 1) * D], 0.0, y1b[:, t * D:(t + 1) * D],
                        ALU.add, ALU.mult, accum_out=lnc[:, 20 + t:21 + t])
                assert ve[0] == VEs[("ln2", g)]
                vsync(vector.tensor_scalar_mul(lnc[:, 24:28], lnc[:, 16:20], 1.0 / D))
                vsync(vector.tensor_tensor(lnc[:, 28:32], lnc[:, 24:28], lnc[:, 24:28], ALU.mult))
                vsync(vector.scalar_tensor_tensor(lnc[:, 20:24], lnc[:, 20:24], 1.0 / D,
                                                  lnc[:, 28:32], ALU.mult, ALU.subtract))
                vinc(vector.tensor_scalar_add(lnc[:, 20:24], lnc[:, 20:24], EPS_LN))
                vector.wait_ge(s_gpc, GP_P2_0 + g * GP_G + GP_G)
                vinc(vector.scalar_tensor_tensor(lnc[:, 24:28], lnc[:, 24:28], -1.0,
                                                 lnc[:, 28:32], ALU.mult, ALU.mult))
            assert ve[0] == VE_END

        @block.scalar
        def _(scalar):
            sc = [0]
            def sinc(inst):
                inst.then_inc(s_sc, 1)
                sc[0] += 1
            for i in range(NT):
                scalar.wait_ge(s_mmp, i + 1)
                scalar.wait_ge(s_ve, i)
                sinc(scalar.activation(e_sb[:], P[0][:], AF.Exp, bias=0.0, scale=1.0))
                sinc(scalar.activation(r_sb[:], P[0][:], AF.Relu, bias=0.0, scale=1.0))
                scalar.wait_ge(s_mmk, i)
                sinc(scalar.activation(v_sb[:], P[1][:], AF.Copy, bias=0.0, scale=1.0))
            assert sc[0] == SC_P1
            scalar.wait_ge(s_ve, 33)
            scalar.dma_start(out=kv_in.rearrange("(p n) -> p n", p=128),
                             in_=kvs[:]).then_inc(s_kvd, 16)

            def emit_qer(gq):
                assert sc[0] == SCs[("qer", gq)]
                eb = e_sb if gq == 0 else e2
                rb = r_sb if gq == 0 else r2
                for co in range(8):
                    scalar.wait_ge(s_mm, MMs[("q", gq)] + co + 1)
                    if co >= 2:
                        scalar.wait_ge(s_ve, VEs[("comb", gq)] + co - 1)
                    half = (co % 2) * 512
                    src = P[2 + (co % 4) // 2][:, (co % 2) * 512:(co % 2) * 512 + 512]
                    sinc(scalar.activation(eb[:, half:half + 512], src, AF.Exp,
                                           bias=bq_sb[:, co:co + 1], scale=1.0))
                    sinc(scalar.activation(rb[:, half:half + 512], src, AF.Relu,
                                           bias=bq_sb[:, co:co + 1], scale=1.0))

            def emit_numden(gq):
                assert sc[0] == SCs[("nd", gq)]
                for p in range(NPAIR):
                    scalar.wait_ge(s_mm, MMs[("nd", gq)] + p + 1)
                    half = (p % 2) * 512
                    sinc(scalar.activation(xT0[:, p * 512:(p + 1) * 512],
                                           P[2][:, half:half + 512], AF.Copy,
                                           bias=0.0, scale=1.0))
                    sinc(scalar.activation(q_elu[:, p * 512:(p + 1) * 512],
                                           P[3][:, half:half + 512], AF.Copy,
                                           bias=0.0, scale=1.0))

            def emit_xf(gq):
                assert sc[0] == SCs[("xf", gq)]
                for t in range(4):
                    scalar.wait_ge(s_ve, VEs[("ln1", gq)] + 2)
                    sinc(scalar.activation(xf[:, t * D:(t + 1) * D],
                                           y1b[:, t * D:(t + 1) * D], AF.Identity,
                                           bias=lnc[:, 8 + t:9 + t],
                                           scale=lnc[:, 12 + t:13 + t]))

            def emit_silu(gq):
                assert sc[0] == SCs[("silu", gq)]
                for idx in range(NIO * 4):
                    c, t = idx // 4, idx % 4
                    io0, iow = IO_CHUNKS[c]
                    half = (idx % 2) * 512
                    pt = P[2 + idx % 2]
                    scalar.wait_ge(s_mm, MMs[("gu", gq)] + idx + 1)
                    if t >= 2:
                        scalar.wait_ge(s_ve, EXTi(gq, c, t - 2))
                    elif c >= 1:
                        scalar.wait_ge(s_ve, EXTi(gq, c - 1, t + 2))
                    sinc(scalar.activation(e_sb[:, half:half + iow], pt[:, 0:iow],
                                           AF.Silu, bias=0.0, scale=1.0))

            def emit_outs(gq):
                assert sc[0] == SCs[("outs", gq)]
                for t in range(4):
                    scalar.wait_ge(s_ve, VEs[("ln2", gq)] + 2)
                    i = gq * 4 + t
                    sinc(scalar.activation(y1b[:, t * D:(t + 1) * D],
                                           y1b[:, t * D:(t + 1) * D], AF.Identity,
                                           bias=lnc[:, 24 + t:25 + t],
                                           scale=lnc[:, 28 + t:29 + t]))
                    scalar.dma_start(out=out_d[i * 128:(i + 1) * 128, :],
                                     in_=y1b[:, t * D:(t + 1) * D]).then_inc(s_st, 16)

            emit_qer(0)
            emit_numden(0)
            emit_xf(0)
            for g in range(NG):
                if g + 1 < NG:
                    emit_qer(g + 1)
                    emit_numden(g + 1)
                emit_silu(g)
                emit_outs(g)
                if g + 1 < NG:
                    emit_xf(g + 1)
            assert sc[0] == SC_END
            scalar.wait_ge(s_st, 16 * NT)

    return nc


_NC_CACHE = None


def kernel(L0_attn_x, L0_attn_Wq, L0_attn_bq, L0_attn_Wk, L0_attn_bk,
           L0_attn_Wv, L0_attn_bv, L0_attn_Wo, L0_attn_bo,
           L0_ffn_w_gate, L0_ffn_w_up, L0_ffn_w_down):
    global _NC_CACHE
    x = np.ascontiguousarray(np.asarray(L0_attn_x))
    B, T, C = x.shape
    assert (B, T, C) == (2, 16384, 1024)
    if _NC_CACHE is None:
        _NC_CACHE = build_nc()
    nc = _NC_CACHE
    in_maps = []
    for core in range(8):
        s, q = core // 4, core % 4
        shard = np.ascontiguousarray(x[s, q * T_CORE:(q + 1) * T_CORE, :])
        in_maps.append({
            "x": shard,
            "Wq": np.asarray(L0_attn_Wq), "Wk": np.asarray(L0_attn_Wk),
            "Wv": np.asarray(L0_attn_Wv), "Wo": np.asarray(L0_attn_Wo),
            "bq": np.asarray(L0_attn_bq), "bk": np.asarray(L0_attn_bk),
            "bv": np.asarray(L0_attn_bv), "bo": np.asarray(L0_attn_bo),
            "wg_sh": np.ascontiguousarray(np.asarray(L0_ffn_w_gate)[core * 352:(core + 1) * 352, :]),
            "wu_sh": np.ascontiguousarray(np.asarray(L0_ffn_w_up)[core * 352:(core + 1) * 352, :]),
            "wd_sh": np.ascontiguousarray(np.asarray(L0_ffn_w_down)[core * 128:(core + 1) * 128, :]),
        })
    res = run_bass_kernel_spmd(nc, in_maps, core_ids=list(range(8)))
    out = np.empty((2, 16384, 1024), dtype=np.float32)
    for core in range(8):
        s, q = core // 4, core % 4
        out[s, q * T_CORE:(q + 1) * T_CORE, :] = res.results[core]["out"]
    return out


# version 15
# speedup vs baseline: 1.0517x; 1.0517x over previous
import contextlib
import numpy as np
import concourse.bass as bass
import concourse.mybir as mybir
from concourse.bass_utils import run_bass_kernel_spmd

F32 = mybir.dt.float32
BF16 = mybir.dt.bfloat16
AF = mybir.ActivationFunctionType
ALU = mybir.AluOpType
AX = mybir.AxisListType

T_CORE = 4096
D = 1024
I_FFN = 2816
NT = 32
NG = 8
NPAIR = 8
EPS_LN = 1e-5
IO_CHUNKS = [(0, 512), (512, 512), (1024, 512), (1536, 512), (2048, 512), (2560, 256)]
NIO = 6
NC22 = 22

SC_P1 = 3 * NT
VE_P2_0 = 34
GP_P2_0 = 17
GP_G = 18

_MM_SIZES = {"q": 8, "nd": 8, "o": 4, "gu": 24, "down": 22}
_MM_ORDER = [("q", 0), ("nd", 0)]
for _g in range(NG):
    _MM_ORDER += [("o", _g)]
    if _g + 1 < NG:
        _MM_ORDER += [("q", _g + 1), ("nd", _g + 1)]
    _MM_ORDER += [("gu", _g), ("down", _g)]
MMs = {}
_c = 0
for _k in _MM_ORDER:
    MMs[_k] = _c
    _c += _MM_SIZES[_k[0]]
assert _c == NG * 66

_SC_SIZES = {"qer": 16, "nd": 16, "outs": 4, "xf": 4, "silu": 24}
_SC_ORDER = [("qer", 0), ("nd", 0), ("xf", 0)]
for _g in range(NG):
    if _g + 1 < NG:
        _SC_ORDER += [("qer", _g + 1), ("nd", _g + 1)]
    _SC_ORDER += [("silu", _g), ("outs", _g)]
    if _g + 1 < NG:
        _SC_ORDER += [("xf", _g + 1)]
SCs = {}
_c = SC_P1
for _k in _SC_ORDER:
    SCs[_k] = _c
    _c += _SC_SIZES[_k[0]]
SC_END = _c

_VE_SIZES = {"comb": 8, "div": 8, "y0": 4, "ln1": 2, "xq": 4, "ext": 36,
             "dev": 4, "ln2": 2}
_VE_ORDER = [("comb", 0), ("div", 0)]
for _g in range(NG):
    _VE_ORDER += [("y0", _g), ("ln1", _g), ("xq", _g)]
    if _g + 1 < NG:
        _VE_ORDER += [("comb", _g + 1)]
    _VE_ORDER += [("ext", _g)]
    if _g + 1 < NG:
        _VE_ORDER += [("div", _g + 1)]
    _VE_ORDER += [("dev", _g), ("ln2", _g)]
VEs = {}
_c = VE_P2_0
for _k in _VE_ORDER:
    VEs[_k] = _c
    _c += _VE_SIZES[_k[0]]
VE_END = _c

def EXTi(g, c, t):
    return VEs[("ext", g)] + 6 * c + (t + 1 if t < 2 else t + 2)
def CHi(g, c, j):
    return VEs[("ext", g)] + 6 * c + (3 if j == 0 else 6)

GM_PRO = 1
def GM_SHLD(k): return 2 + k
def GM_XQ(g): return 10 + 7 * g
def GM_HQC(g, c): return 11 + 7 * g + c

PT_SPECS = ([("g", c) for c in range(8)] + [("u", c) for c in range(8)]
            + [("d", c) for c in range(NC22)])
PT_CC = {"g": 1, "u": 2, "d": 3}
NPT = len(PT_SPECS)

SHARD_TILES = [
    ("g", 0, 128, D), ("g", 128, 128, D), ("g", 256, 96, D),
    ("u", 0, 128, D), ("u", 128, 128, D), ("u", 256, 96, D),
    ("d", 0, 128, 1408), ("d", 1408, 128, 1408),
]
SHARD_CHAINS = [
    (0, 128, 0, D, ("g", 0)), (1, 128, 0, D, ("g", 128)), (2, 96, 0, D, ("g", 256)),
    (3, 128, 0, D, ("u", 0)), (4, 128, 0, D, ("u", 128)), (5, 96, 0, D, ("u", 256)),
    (6, 128, 0, 1024, None), (6, 128, 1024, 384, ("dcol", 0)),
    (7, 128, 0, 1024, None), (7, 128, 1024, 384, ("dcol", 1408)),
]
NCH = 10


def emit_quant_dve(vector, x_ap, nblk, F, qs, vsync):
    P = x_ap.shape[0]
    bm = qs["bm"][0:P, 0:nblk]
    sc = qs["sc"][0:P, 0:nblk]
    rcp = qs["rcp"][0:P, 0:nblk]
    y = qs["y"][0:P, 0:F]
    c2 = qs["c2"][0:P, 0:F]
    w4 = qs["w4"][0:P, 0:F]
    t16 = qs["t16"][0:P, 0:F]
    r4 = qs["r4"][0:P, 0:F]
    d2 = qs["d2"][0:P, 0:F]
    vsync(vector.tensor_reduce(bm, x_ap.rearrange("p (b k) -> p b k", k=16), AX.X,
                               ALU.max, apply_absolute_value=True))
    vsync(vector.tensor_scalar(sc, bm, 1.0 / 6.0, 1e-12, ALU.mult, ALU.max))
    vsync(vector.reciprocal(rcp, sc))
    vsync(vector.tensor_scalar_mul(sc, sc, 0.25))
    rcp_rep = rcp.unsqueeze(2).broadcast_to([P, nblk, 16])
    vector.tensor_tensor(y.rearrange("p (b k) -> p b k", k=16),
                         x_ap.rearrange("p (b k) -> p b k", k=16), rcp_rep, ALU.mult)
    vector.tensor_scalar(c2, y, 2.0, -2.0, ALU.min, ALU.max)
    vector.scalar_tensor_tensor(w4, c2, 2.0, y, ALU.mult, ALU.add)
    vector.tensor_scalar(c2, y, 4.0, -4.0, ALU.min, ALU.max)
    vector.tensor_tensor(w4, w4, c2, ALU.add)
    vector.tensor_scalar_mul(t16, w4, 0.5)
    vector.tensor_scalar_mul(r4, t16, 2.0)
    vector.tensor_scalar(d2, r4, 8.0, -8.0, ALU.min, ALU.max)
    vector.scalar_tensor_tensor(d2, r4, 4.0, d2, ALU.mult, ALU.subtract)
    vector.tensor_scalar(r4, r4, 12.0, -12.0, ALU.min, ALU.max)
    vector.scalar_tensor_tensor(r4, r4, -2.0, d2, ALU.mult, ALU.add)


def emit_quant_gp_ret(gpsimd, q_out, nblk, F, qs):
    Pn = q_out.shape[0]
    sc_rep = qs["sc"][0:Pn, 0:nblk].unsqueeze(2).broadcast_to([Pn, nblk, 16])
    return gpsimd.tensor_tensor(q_out.rearrange("p (b k) -> p b k", k=16),
                                qs["r4"][0:Pn, 0:F].rearrange("p (b k) -> p b k", k=16),
                                sc_rep, ALU.mult)


def build_nc():
    nc = bass.Bass("TRN2", target_bir_lowering=False, debug=False,
                   detect_race_conditions=False)
    x_d = nc.dram_tensor("x", [T_CORE, D], F32, kind="ExternalInput").ap()
    w_in = {k: nc.dram_tensor("W" + k, [D, D], F32, kind="ExternalInput").ap()
            for k in ("q", "k", "v", "o")}
    bq_d = nc.dram_tensor("bq", [D], F32, kind="ExternalInput").ap()
    bk_d = nc.dram_tensor("bk", [D], F32, kind="ExternalInput").ap()
    bv_d = nc.dram_tensor("bv", [D], F32, kind="ExternalInput").ap()
    bo_d = nc.dram_tensor("bo", [D], F32, kind="ExternalInput").ap()
    wg_sh_d = nc.dram_tensor("wg_sh", [I_FFN // 8, D], F32, kind="ExternalInput").ap()
    wu_sh_d = nc.dram_tensor("wu_sh", [I_FFN // 8, D], F32, kind="ExternalInput").ap()
    wd_sh_d = nc.dram_tensor("wd_sh", [D // 8, I_FFN], F32, kind="ExternalInput").ap()
    out_d = nc.dram_tensor("out", [T_CORE, D], F32, kind="ExternalOutput").ap()

    xbf_d = nc.dram_tensor("xbf_d", [T_CORE, D], BF16, kind="Internal").ap()
    wbf = {k: nc.dram_tensor("wbf_" + k, [D, D], BF16, kind="Internal").ap()
           for k in ("q", "k", "v", "o")}
    qin = {m: nc.dram_tensor("q%s_in" % m, [I_FFN // 8 * D if m != "d" else D // 8 * I_FFN],
                             BF16, kind="Internal").ap() for m in ("g", "u", "d")}
    qout = {m: nc.dram_tensor("q%s_out" % m, [I_FFN * D], BF16, kind="Internal").ap()
            for m in ("g", "u", "d")}
    wgT_d = nc.dram_tensor("wgT_d", [D, I_FFN], BF16, kind="Internal").ap()
    wuT_d = nc.dram_tensor("wuT_d", [D, I_FFN], BF16, kind="Internal").ap()
    wdT_d = nc.dram_tensor("wdT_d", [I_FFN, D], BF16, kind="Internal").ap()
    kv_in = nc.dram_tensor("kv_in", [128 * 1032], F32, kind="Internal").ap()
    kv_out = nc.dram_tensor("kv_out", [128 * 1032], F32, kind="Internal").ap()
    xq_d = nc.dram_tensor("xq_d", [512, D], BF16, kind="Internal").ap()
    hq_d = nc.dram_tensor("hq_d", [512, I_FFN], BF16, kind="Internal").ap()

    LM = {}
    L_PT = []

    with contextlib.ExitStack() as stack:
        E = stack.enter_context
        wA = E(nc.sbuf_tensor("wA", [128, 8 * D], BF16))
        wB = E(nc.sbuf_tensor("wB", [128, 8 * D], BF16))
        xT0 = E(nc.sbuf_tensor("xT0", [128, 8 * 512], BF16))
        xT1 = E(nc.sbuf_tensor("xT1", [128, 8 * 512], BF16))
        k_sb = E(nc.sbuf_tensor("k_sb", [128, D], BF16))
        v_sb = E(nc.sbuf_tensor("v_sb", [128, D], BF16))
        e_sb = E(nc.sbuf_tensor("e_sb", [128, D], F32))
        r_sb = E(nc.sbuf_tensor("r_sb", [128, D], F32))
        kvs = E(nc.sbuf_tensor("kvs", [128, 1032], F32))
        kvbd = E(nc.sbuf_tensor("kvbd", [128, NPAIR * 128], BF16))
        ksr = E(nc.sbuf_tensor("ksr", [128, NPAIR * 128], BF16))
        q_elu = E(nc.sbuf_tensor("q_elu", [128, 8 * 512], BF16))
        xf = E(nc.sbuf_tensor("xf", [128, 4 * D], F32))
        y1b = E(nc.sbuf_tensor("y1b", [128, 4 * D], F32))
        y0s = E(nc.sbuf_tensor("y0s", [128, D], F32))
        sqs = E(nc.sbuf_tensor("sqs", [128, D], F32))
        hq_sb = E(nc.sbuf_tensor("hq_sb", [128, 4 * I_FFN], BF16))
        wgu = E(nc.sbuf_tensor("wgu", [128, 2 * 2 * 8 * 512], BF16))
        wdt = E(nc.sbuf_tensor("wdt", [128, 2 * D], BF16))
        qs = {
            "bm": E(nc.sbuf_tensor("q_bm", [128, 64], F32)),
            "sc": E(nc.sbuf_tensor("q_sc", [128, 64], F32)),
            "rcp": E(nc.sbuf_tensor("q_rcp", [128, 64], F32)),
            "y": E(nc.sbuf_tensor("q_y", [128, 1024], F32)),
            "c2": E(nc.sbuf_tensor("q_c2", [128, 1024], F32)),
            "w4": E(nc.sbuf_tensor("q_w4", [128, 1024], F32)),
            "t16": E(nc.sbuf_tensor("q_t16", [128, 1024], mybir.dt.int16)),
            "r4": E(nc.sbuf_tensor("q_r4", [128, 1024], BF16)),
            "d2": E(nc.sbuf_tensor("q_d2", [128, 1024], BF16)),
        }
        shw = E(nc.sbuf_tensor("shw", [128, 1408], F32))
        shq = E(nc.sbuf_tensor("shq", [128, 1408], BF16))
        lnc = E(nc.sbuf_tensor("lnc", [128, 32], F32))
        ones1 = E(nc.sbuf_tensor("ones1", [1, 128], BF16))
        onecol = E(nc.sbuf_tensor("onecol", [128, 1], BF16))
        mhalf = E(nc.sbuf_tensor("mhalf", [128, 1], F32))
        brows = E(nc.sbuf_tensor("brows", [1, 3 * D], BF16))
        bq_sb = E(nc.sbuf_tensor("bq_sb", [128, 8], F32))
        P = [E(nc.psum_tensor("P%d" % i, [128, 1024], F32)) for i in range(4)]

        e2 = kvs[:, 0:1024]
        r2 = shw[:, 0:1024]

        s_ld = E(nc.semaphore("s_ld"))
        s_gpd = E(nc.semaphore("s_gpd"))
        s_gpc = E(nc.semaphore("s_gpc"))
        s_cc = E(nc.semaphore("s_cc"))
        s_mmp = E(nc.semaphore("s_mmp"))
        s_mmk = E(nc.semaphore("s_mmk"))
        s_mm = E(nc.semaphore("s_mm"))
        s_ve = E(nc.semaphore("s_ve"))
        s_vq = E(nc.semaphore("s_vq"))
        s_sc = E(nc.semaphore("s_sc"))
        s_st = E(nc.semaphore("s_st"))
        s_gm = E(nc.semaphore("s_gm"))
        s_lm = E(nc.semaphore("s_lm"))
        s_vs = E(nc.semaphore("s_vs"))
        s_pt = E(nc.semaphore("s_pt"))
        s_kvd = E(nc.semaphore("s_kvd"))
        s_wd = E(nc.semaphore("s_wd"))
        WD = {(g, c): 16 * (11 * g + (c - 1) // 2 + 1)
              for g in range(NG) for c in range(1, NC22, 2)}

        block = E(nc.Block())

        @block.sync
        def _(sync):
            ld = 0
            lm = 0
            def dma(out, in_, transpose=False, slow=False):
                nonlocal ld
                sync.dma_start(out=out, in_=in_, transpose=transpose,
                               allow_slow_non_contiguous=slow).then_inc(s_ld, 16)
                ld += 16
            def milestone(key):
                nonlocal lm
                sync.wait_ge(s_ld, ld)
                sync.sem_inc(s_lm, 1)
                lm += 1
                LM[key] = lm
            def load_xT0(g):
                for c in range(8):
                    dma(xT0[:, c * 512:(c + 1) * 512],
                        xbf_d[g * 512:(g + 1) * 512, c * 128:(c + 1) * 128],
                        transpose=True)
                milestone(("xT0", g))
            def load_xf(g):
                for t in range(4):
                    dma(xf[:, t * D:(t + 1) * D],
                        x_d[g * 512 + t * 128: g * 512 + (t + 1) * 128, :])
                milestone(("xf", g))
            dma(bq_sb[:], bq_d.rearrange("(c p) -> p c", p=128), slow=True)
            sync.wait_ge(s_gm, GM_PRO)
            for name, dst in (("k", wA), ("v", wB)):
                for c in range(8):
                    dma(dst[:, c * D:(c + 1) * D],
                        wbf[name][:, c * 128:(c + 1) * 128], transpose=True)
            milestone("wkv")
            for g in range(NG):
                if g >= 2:
                    sync.wait_ge(s_mmp, 4 * (g - 1))
                buf = xT0 if g % 2 == 0 else xT1
                for c in range(8):
                    dma(buf[:, c * 512:(c + 1) * 512],
                        xbf_d[g * 512:(g + 1) * 512, c * 128:(c + 1) * 128],
                        transpose=True)
                milestone(("p1x", g))
            for j, (m, c) in enumerate(PT_SPECS):
                sync.wait_ge(s_cc, PT_CC[m])
                if j >= 4:
                    sync.wait_ge(s_pt, j - 3)
                rows = I_FFN if m in ("g", "u") else D
                cols = D if m in ("g", "u") else I_FFN
                srcm = qout[m][0:I_FFN * D].rearrange("(r c) -> r c", c=cols)
                dma(hq_sb[:, (j % 4) * I_FFN:(j % 4) * I_FFN + rows],
                    srcm[:, c * 128:(c + 1) * 128], transpose=True)
                L_PT.append(ld)
            sync.wait_ge(s_mmk, NT)
            for name, dst in (("q", wA), ("o", wB)):
                for c in range(8):
                    dma(dst[:, c * D:(c + 1) * D],
                        wbf[name][:, c * 128:(c + 1) * 128], transpose=True)
            milestone("wqo")
            load_xT0(0)
            load_xf(0)
            sync.wait_ge(s_cc, 4)
            dma(kvs[:], kv_out.rearrange("(p n) -> p n", p=128))
            milestone("kvrel")
            for g in range(NG):
                for c in range(2):
                    if g == 0:
                        sync.wait_ge(s_pt, 16)
                    else:
                        sync.wait_ge(s_mm, MMs[("gu", g - 1)] + (20 if c == 0 else 24))
                    io0, iow = IO_CHUNKS[c]
                    h0 = (c % 2) * 8192
                    dma(wgu[:, h0:h0 + 4096].rearrange("p (a w) -> p a w", w=512)[:, :, 0:iow],
                        wgT_d.rearrange("(a p) io -> p a io", p=128)[:, :, io0:io0 + iow])
                    dma(wgu[:, h0 + 4096:h0 + 8192].rearrange("p (a w) -> p a w", w=512)[:, :, 0:iow],
                        wuT_d.rearrange("(a p) io -> p a io", p=128)[:, :, io0:io0 + iow])
                    milestone(("guc", g, c))
                if g + 1 < NG:
                    sync.wait_ge(s_ve, VEs[("div", g)] + 8)
                    load_xT0(g + 1)
                if g == 0:
                    sync.wait_ge(s_pt, 17)
                else:
                    sync.wait_ge(s_mm, MMs[("down", g - 1)] + 21)
                dma(wdt[:, 0:D], wdT_d[0:128, :])
                milestone(("dn", g, 0))
                if g >= 1:
                    sync.wait_ge(s_ve, VEs[("ln2", g - 1)] + 2)
                    load_xf(g)
                sync.wait_ge(s_gm, GM_XQ(g))
                for c in range(8):
                    dma(xT1[:, c * 512:(c + 1) * 512],
                        xq_d[:, c * 128:(c + 1) * 128], transpose=True)
                milestone(("xqT", g))
                for c in range(2, NIO):
                    io0, iow = IO_CHUNKS[c]
                    h0 = (c % 2) * 8192
                    sync.wait_ge(s_mm, MMs[("gu", g)] + 4 * (c - 1))
                    dma(wgu[:, h0:h0 + 4096].rearrange("p (a w) -> p a w", w=512)[:, :, 0:iow],
                        wgT_d.rearrange("(a p) io -> p a io", p=128)[:, :, io0:io0 + iow])
                    dma(wgu[:, h0 + 4096:h0 + 8192].rearrange("p (a w) -> p a w", w=512)[:, :, 0:iow],
                        wuT_d.rearrange("(a p) io -> p a io", p=128)[:, :, io0:io0 + iow])
                    milestone(("guc", g, c))
                nco = 0
                for c in range(NIO):
                    io0, iow = IO_CHUNKS[c]
                    sync.wait_ge(s_gm, GM_HQC(g, c))
                    for cc in range(io0 // 128, (io0 + iow) // 128):
                        dma(hq_sb[:, cc * 512:(cc + 1) * 512],
                            hq_d[:, cc * 128:(cc + 1) * 128], transpose=True)
                        nco += 1
                assert nco == NC22
                milestone(("hqT", g))
                for c in range(2, NC22, 2):
                    sync.wait_ge(s_mm, MMs[("down", g)] + (c - 2) + 1)
                    dma(wdt[:, 0:D], wdT_d[c * 128:(c + 1) * 128, :])
                    milestone(("dnc", g, c))

        @block.gpsimd
        def _(gpsimd):
            gpd = 0
            gpc = 0
            gm = 0
            def gdma(out, in_):
                nonlocal gpd
                gpsimd.dma_start(out=out, in_=in_).then_inc(s_gpd, 16)
                gpd += 16
            def ginc(inst):
                nonlocal gpc
                inst.then_inc(s_gpc, 1)
                gpc += 1
            def gmilestone(expect=None):
                nonlocal gm
                gpsimd.wait_ge(s_gpd, gpd)
                gpsimd.sem_inc(s_gm, 1)
                gm += 1
                if expect is not None:
                    assert gm == expect, (gm, expect)
            ginc(gpsimd.memset(ones1[:], 1.0))
            ginc(gpsimd.memset(onecol[:], 1.0))
            ginc(gpsimd.memset(mhalf[:], -0.5))
            ginc(gpsimd.memset(kvbd[:], 0.0))
            ginc(gpsimd.memset(ksr[:], 0.0))
            ginc(gpsimd.engine_nop())
            ginc(gpsimd.engine_nop())
            assert gpc == 7
            for j in range(4):
                gdma(xbf_d[j * 1024:(j + 1) * 1024, :], x_d[j * 1024:(j + 1) * 1024, :])
            for name in ("q", "k", "v", "o"):
                gdma(wbf[name], w_in[name])
            for j, b in enumerate((bk_d, bv_d, bo_d)):
                gdma(brows[:, j * D:(j + 1) * D], b[None, :])
            gmilestone(GM_PRO)
            def shard_load(k):
                m, r0, nr, w = SHARD_TILES[k]
                if m == "d":
                    gdma(shw[0:128, 0:w], wd_sh_d[:, r0:r0 + w])
                else:
                    srct = {"g": wg_sh_d, "u": wu_sh_d}[m]
                    gdma(shw[0:nr, 0:w], srct[r0:r0 + nr, :])
            shard_load(0)
            gmilestone(GM_SHLD(0))
            for j, (li, nr, c0, w, wr) in enumerate(SHARD_CHAINS):
                gpsimd.wait_ge(s_vq, j + 1)
                if j + 1 < len(SHARD_CHAINS) and SHARD_CHAINS[j + 1][0] == li + 1:
                    shard_load(li + 1)
                    gmilestone(GM_SHLD(li + 1))
                gpsimd.wait_ge(s_gpd, gpd)
                ginc(emit_quant_gp_ret(gpsimd, shq[0:nr, c0:c0 + w], w // 16, w, qs))
                if wr is not None:
                    kind, base = wr
                    if kind == "dcol":
                        gdma(qin["d"][0:128 * I_FFN].rearrange("(r c) -> r c", c=I_FFN)[:, base:base + 1408],
                             shq[0:128, 0:1408])
                    else:
                        gdma(qin[kind][base * D:(base + nr) * D].rearrange("(r c) -> r c", c=D),
                             shq[0:nr, 0:D])
            gpsimd.wait_ge(s_gpd, gpd)
            assert gpc == GP_P2_0
            gpsimd.collective_compute("AllGather", ALU.bypass,
                                      replica_groups=[[0, 1, 2, 3, 4, 5, 6, 7]],
                                      ins=[qin["g"]], outs=[qout["g"]]).then_inc(s_cc, 1)
            gpsimd.collective_compute("AllGather", ALU.bypass,
                                      replica_groups=[[0, 1, 2, 3, 4, 5, 6, 7]],
                                      ins=[qin["u"]], outs=[qout["u"]]).then_inc(s_cc, 1)
            gpsimd.collective_compute("AllGather", ALU.bypass,
                                      replica_groups=[[0, 1, 2, 3, 4, 5, 6, 7]],
                                      ins=[qin["d"]], outs=[qout["d"]]).then_inc(s_cc, 1)
            gpd_after = []
            for j, (m, c) in enumerate(PT_SPECS):
                gpsimd.wait_ge(s_ld, L_PT[j])
                rows = I_FFN if m in ("g", "u") else D
                dstT = {"g": wgT_d, "u": wuT_d, "d": wdT_d}[m]
                gdma(dstT[c * 128:(c + 1) * 128, :],
                     hq_sb[:, (j % 4) * I_FFN:(j % 4) * I_FFN + rows])
                gpd_after.append(gpd)
                if j >= 1:
                    gpsimd.wait_ge(s_gpd, gpd_after[j - 1])
                    gpsimd.sem_inc(s_pt, 1)
            gpsimd.wait_ge(s_gpd, gpd_after[-1])
            gpsimd.sem_inc(s_pt, 1)
            gpsimd.wait_ge(s_kvd, 16)
            gpsimd.collective_compute("AllReduce", ALU.add,
                                      replica_groups=[[0, 1, 2, 3], [4, 5, 6, 7]],
                                      ins=[kv_in], outs=[kv_out]).then_inc(s_cc, 1)
            for g in range(NG):
                base_gp = GP_P2_0 + g * GP_G
                gpsimd.wait_ge(s_ve, VEs[("ln1", g)] + 1)
                ginc(gpsimd.tensor_tensor(lnc[:, 12:16], lnc[:, 4:8],
                                          mhalf[:].broadcast_to([128, 4]), ALU.pow))
                assert gpc == base_gp + 1
                for t in range(4):
                    gpsimd.wait_ge(s_ve, VEs[("xq", g)] + t + 1)
                    ginc(emit_quant_gp_ret(gpsimd, xT1[:, t * D:(t + 1) * D], 64, D, qs))
                assert gpc == base_gp + 5
                gdma(xq_d.rearrange("(t p) c -> p t c", p=128),
                     xT1[:].rearrange("p (t c) -> p t c", c=D))
                gmilestone(GM_XQ(g))
                for c in range(NIO):
                    io0, iow = IO_CHUNKS[c]
                    for j in range(2):
                        gpsimd.wait_ge(s_ve, CHi(g, c, j))
                        blk = 4 * io0 + 2 * j * iow
                        ginc(emit_quant_gp_ret(
                            gpsimd, hq_sb[:, blk:blk + 2 * iow], 2 * iow // 16,
                            2 * iow, qs))
                    gdma(hq_d.rearrange("(t p) z -> p t z", p=128)[:, :, io0:io0 + iow],
                         hq_sb[:, 4 * io0:4 * (io0 + iow)].rearrange(
                             "p (t w) -> p t w", w=iow))
                    gmilestone(GM_HQC(g, c))
                gpsimd.wait_ge(s_ve, VEs[("ln2", g)] + 1)
                ginc(gpsimd.tensor_tensor(lnc[:, 28:32], lnc[:, 20:24],
                                          mhalf[:].broadcast_to([128, 4]), ALU.pow))
                assert gpc == base_gp + GP_G

        @block.tensor
        def _(tensor):
            mmp = mmk = 0
            mm = [0]
            def minc(inst):
                inst.then_inc(s_mm, 1)
                mm[0] += 1
            for i in range(NT + 1):
                if i < NT:
                    g, tl = i // 4, i % 4
                    if i == 0:
                        tensor.wait_ge(s_gpc, 7)
                        tensor.wait_ge(s_gm, GM_PRO)
                        tensor.wait_ge(s_lm, LM["wkv"])
                    if tl == 0:
                        tensor.wait_ge(s_lm, LM[("p1x", g)])
                    tensor.wait_ge(s_sc, 3 * i)
                    xbuf = xT0 if g % 2 == 0 else xT1
                    for dst, wT, bias_off in ((P[0], wA, 0), (P[1], wB, D)):
                        for h in range(2):
                            for cf in range(8):
                                tensor.matmul(
                                    dst[:, h * 512:(h + 1) * 512],
                                    xbuf[:, cf * 512 + tl * 128: cf * 512 + (tl + 1) * 128],
                                    wT[:, cf * D + h * 512: cf * D + (h + 1) * 512],
                                    start=(cf == 0), stop=False)
                            last = tensor.matmul(dst[:, h * 512:(h + 1) * 512], ones1[:],
                                          brows[:, bias_off + h * 512: bias_off + (h + 1) * 512],
                                          start=False, stop=True)
                    last.then_inc(s_mmp, 1)
                    mmp += 1
                if i >= 1:
                    j = i - 1
                    tensor.wait_ge(s_ve, j + 1)
                    if i == NT:
                        tensor.wait_ge(s_sc, 3 * NT)
                    for p in range(NPAIR):
                        tensor.matmul(P[2][:, p * 128:(p + 1) * 128],
                                      k_sb[:, p * 128:(p + 1) * 128],
                                      v_sb[:, p * 128:(p + 1) * 128],
                                      start=(j == 0 and p % 4 == 0),
                                      stop=(j == NT - 1 and p % 4 == 3))
                        last = tensor.matmul(P[3][:, p:p + 1],
                                      k_sb[:, p * 128:(p + 1) * 128],
                                      onecol[:],
                                      start=(j == 0 and p == 0),
                                      stop=(j == NT - 1 and p == 7))
                    last.then_inc(s_mmk, 1)
                    mmk += 1
            assert mmp == NT and mmk == NT

            def emit_q(gq):
                assert mm[0] == MMs[("q", gq)]
                tensor.wait_ge(s_lm, LM[("xT0", gq)])
                for co in range(8):
                    if co == 0:
                        if gq == 0:
                            tensor.wait_ge(s_ve, 33)
                        else:
                            tensor.wait_ge(s_ve, VEs[("y0", gq - 1)] + 4)
                    if co >= 4:
                        tensor.wait_ge(s_sc, SCs[("qer", gq)] + 2 * (co - 3))
                    pdst = P[2 + (co % 4) // 2]
                    for cf in range(8):
                        last = tensor.matmul(pdst[:, (co % 2) * 512:(co % 2) * 512 + 512],
                                      wA[:, cf * D + co * 128: cf * D + (co + 1) * 128],
                                      xT0[:, cf * 512:(cf + 1) * 512],
                                      start=(cf == 0), stop=(cf == 7))
                    minc(last)

            def emit_nd(gq):
                assert mm[0] == MMs[("nd", gq)]
                for p in range(NPAIR):
                    tensor.wait_ge(s_ve, VEs[("comb", gq)] + p + 1)
                    if p == 0:
                        tensor.wait_ge(s_sc, SCs[("qer", gq)] + 14)
                    elif p == 1:
                        tensor.wait_ge(s_sc, SCs[("qer", gq)] + 16)
                    else:
                        tensor.wait_ge(s_sc, SCs[("nd", gq)] + 2 * (p - 1))
                    half = (p % 2) * 512
                    tensor.matmul(P[2][:, half:half + 512],
                                  kvbd[:, p * 128:(p + 1) * 128],
                                  q_elu[:, p * 512:(p + 1) * 512],
                                  start=True, stop=True)
                    last = tensor.matmul(P[3][:, half:half + 512],
                                  ksr[:, p * 128:(p + 1) * 128],
                                  q_elu[:, p * 512:(p + 1) * 512],
                                  start=True, stop=True)
                    minc(last)

            emit_q(0)
            emit_nd(0)
            for g in range(NG):
                assert mm[0] == MMs[("o", g)]
                for t in range(4):
                    if t == 0:
                        if g == 0:
                            tensor.wait_ge(s_ve, VEs[("div", 0)] + 8)
                        else:
                            tensor.wait_ge(s_ve, VEs[("dev", g - 1)] + 3)
                    else:
                        tensor.wait_ge(s_ve, VEs[("y0", g)] + t)
                    for h in range(2):
                        for p in range(NPAIR):
                            tensor.matmul(P[2][:, h * 512:(h + 1) * 512],
                                          q_elu[:, p * 512 + t * 128: p * 512 + (t + 1) * 128],
                                          wB[:, p * D + h * 512: p * D + (h + 1) * 512],
                                          start=(p == 0), stop=False)
                        last = tensor.matmul(P[2][:, h * 512:(h + 1) * 512], ones1[:],
                                      brows[:, 2 * D + h * 512: 2 * D + (h + 1) * 512],
                                      start=False, stop=True)
                    minc(last)
                if g + 1 < NG:
                    emit_q(g + 1)
                    emit_nd(g + 1)
                assert mm[0] == MMs[("gu", g)]
                tensor.wait_ge(s_lm, LM[("xqT", g)])
                if g + 1 < NG:
                    tensor.wait_ge(s_sc, SCs[("nd", g + 1)] + 16)
                for idx in range(NIO * 4):
                    c, t = idx // 4, idx % 4
                    io0, iow = IO_CHUNKS[c]
                    h0 = (c % 2) * 8192
                    if t == 0:
                        tensor.wait_ge(s_lm, LM[("guc", g, c)])
                    if t >= 2:
                        tensor.wait_ge(s_ve, EXTi(g, c, t - 2))
                    elif c >= 1:
                        tensor.wait_ge(s_ve, EXTi(g, c - 1, t + 2))
                    pt = P[2 + idx % 2]
                    for cf in range(8):
                        tensor.matmul(pt[:, 0:iow],
                                      xT1[:, cf * 512 + t * 128: cf * 512 + (t + 1) * 128],
                                      wgu[:, h0 + cf * 512: h0 + cf * 512 + iow],
                                      start=(cf == 0), stop=(cf == 7))
                    for cf in range(8):
                        last = tensor.matmul(pt[:, 512:512 + iow],
                                      xT1[:, cf * 512 + t * 128: cf * 512 + (t + 1) * 128],
                                      wgu[:, h0 + 4096 + cf * 512: h0 + 4096 + cf * 512 + iow],
                                      start=(cf == 0), stop=(cf == 7))
                    minc(last)
                assert mm[0] == MMs[("down", g)]
                tensor.wait_ge(s_ve, VEs[("ext", g)] + 36)
                tensor.wait_ge(s_lm, LM[("hqT", g)])
                for c in range(NC22):
                    if c % 2 == 0:
                        tensor.wait_ge(s_lm, LM[("dn", g, 0) if c == 0 else ("dnc", g, c)])
                    else:
                        tensor.wait_ge(s_wd, WD[(g, c)])
                    for t in range(4):
                        for h in range(2):
                            last = tensor.matmul(P[t][:, h * 512:(h + 1) * 512],
                                          hq_sb[:, c * 512 + t * 128: c * 512 + (t + 1) * 128],
                                          wdt[:, (c % 2) * D + h * 512: (c % 2) * D + (h + 1) * 512],
                                          start=(c == 0), stop=(c == NC22 - 1))
                    minc(last)
            assert mm[0] == NG * 66

        @block.vector
        def _(vector):
            ve = [0]
            vq = 0
            vs_n = [0]
            def vinc(inst):
                inst.then_inc(s_ve, 1)
                ve[0] += 1
            def vsync(inst):
                vs_n[0] += 1
                inst.then_inc(s_vs, 1)
                vector.wait_ge(s_vs, vs_n[0])
            for i in range(NT):
                vector.wait_ge(s_mmk, i)
                vector.wait_ge(s_sc, 3 * i + 2)
                vinc(vector.scalar_tensor_tensor(k_sb[:], e_sb[:], 1.0, r_sb[:],
                                                 ALU.min, ALU.add))
                assert ve[0] == i + 1
                if i < NCH:
                    li, nr, c0, w, _wr = SHARD_CHAINS[i]
                    vector.wait_ge(s_gm, GM_SHLD(li))
                    vector.wait_ge(s_gpc, 7 + i)
                    emit_quant_dve(vector, shw[0:nr, c0:c0 + w], w // 16, w, qs, vsync)
                    vector.engine_nop().then_inc(s_vq, 1)
                    vq += 1
            vector.wait_ge(s_mmk, NT)
            vector.tensor_copy(kvs[:, 0:512], P[2][:, 0:512])
            vector.tensor_copy(kvs[:, 512:1024], P[2][:, 512:1024])
            vinc(vector.tensor_copy(kvs[:, 1024:1032], P[3][:, 0:8]))
            assert ve[0] == 33
            vector.wait_ge(s_lm, LM["kvrel"])
            vector.wait_ge(s_gpc, 7)
            for p in range(NPAIR):
                src0 = kvs[0:64, p * 128: p * 128 + 64]
                src1 = kvs[64:128, p * 128 + 64: p * 128 + 128]
                vector.tensor_copy(kvbd[0:64, p * 128: p * 128 + 64], src0)
                vector.tensor_copy(kvbd[64:128, p * 128 + 64: p * 128 + 128], src1)
                ks = kvs[:, 1024 + p:1025 + p]
                vector.tensor_copy(ksr[0:64, p * 128: p * 128 + 64],
                                   ks[0:64, :].broadcast_to([64, 64]))
                vector.tensor_copy(ksr[64:128, p * 128 + 64: p * 128 + 128],
                                   ks[64:128, :].broadcast_to([64, 64]))
            vinc(vector.engine_nop())
            assert ve[0] == VE_P2_0

            def emit_comb(gq):
                assert ve[0] == VEs[("comb", gq)]
                eb = e_sb if gq == 0 else e2
                rb = r_sb if gq == 0 else r2
                for co in range(8):
                    vector.wait_ge(s_sc, SCs[("qer", gq)] + 2 * (co + 1))
                    half = (co % 2) * 512
                    vinc(vector.scalar_tensor_tensor(
                        q_elu[:, co * 512:(co + 1) * 512],
                        eb[:, half:half + 512], 1.0, rb[:, half:half + 512],
                        ALU.min, ALU.add))

            def emit_div(gq):
                assert ve[0] == VEs[("div", gq)]
                for p in range(NPAIR):
                    vector.wait_ge(s_sc, SCs[("nd", gq)] + 2 * (p + 1))
                    vector.tensor_scalar_max(sqs[:, 0:512],
                                             q_elu[:, p * 512:(p + 1) * 512], 1e-6)
                    vector.reciprocal(sqs[:, 512:1024], sqs[:, 0:512])
                    vinc(vector.tensor_tensor(q_elu[:, p * 512:(p + 1) * 512],
                                              xT0[:, p * 512:(p + 1) * 512],
                                              sqs[:, 512:1024], ALU.mult))

            emit_comb(0)
            emit_div(0)
            for g in range(NG):
                assert ve[0] == VEs[("y0", g)]
                vector.wait_ge(s_lm, LM[("xf", g)])
                for t in range(4):
                    vector.wait_ge(s_mm, MMs[("o", g)] + t + 1)
                    if g >= 1:
                        vector.wait_ge(s_sc, SCs[("outs", g - 1)] + t + 1)
                        vector.wait_ge(s_st, 16 * (4 * (g - 1) + t + 1))
                    vinc(vector.scalar_tensor_tensor(
                        y1b[:, t * D:(t + 1) * D], P[2][:], 0.0, xf[:, t * D:(t + 1) * D],
                        ALU.add, ALU.add, accum_out=lnc[:, t:t + 1]))
                    vector.scalar_tensor_tensor(
                        sqs[:], y1b[:, t * D:(t + 1) * D], 0.0, y1b[:, t * D:(t + 1) * D],
                        ALU.add, ALU.mult, accum_out=lnc[:, 4 + t:5 + t])
                assert ve[0] == VEs[("ln1", g)]
                vsync(vector.tensor_scalar_mul(lnc[:, 8:12], lnc[:, 0:4], 1.0 / D))
                vsync(vector.tensor_tensor(lnc[:, 12:16], lnc[:, 8:12], lnc[:, 8:12], ALU.mult))
                vsync(vector.scalar_tensor_tensor(lnc[:, 4:8], lnc[:, 4:8], 1.0 / D,
                                                  lnc[:, 12:16], ALU.mult, ALU.subtract))
                vinc(vector.tensor_scalar_add(lnc[:, 4:8], lnc[:, 4:8], EPS_LN))
                vector.wait_ge(s_gpc, GP_P2_0 + g * GP_G + 1)
                vinc(vector.scalar_tensor_tensor(lnc[:, 8:12], lnc[:, 8:12], -1.0,
                                                 lnc[:, 12:16], ALU.mult, ALU.mult))
                assert ve[0] == VEs[("xq", g)]
                for t in range(4):
                    vector.wait_ge(s_sc, SCs[("xf", g)] + t + 1)
                    vector.wait_ge(s_gpc, GP_P2_0 + g * GP_G + 1 + t)
                    emit_quant_dve(vector, xf[:, t * D:(t + 1) * D], 64, D, qs, vsync)
                    vinc(vector.engine_nop())
                if g + 1 < NG:
                    emit_comb(g + 1)
                assert ve[0] == VEs[("ext", g)]
                for idx in range(NIO * 4):
                    c, t = idx // 4, idx % 4
                    io0, iow = IO_CHUNKS[c]
                    half = (idx % 2) * 512
                    pt = P[2 + idx % 2]
                    vector.wait_ge(s_sc, SCs[("silu", g)] + idx + 1)
                    vinc(vector.tensor_tensor(y0s[:, (t % 2) * iow:(t % 2 + 1) * iow],
                                              e_sb[:, half:half + iow],
                                              pt[:, 512:512 + iow], ALU.mult))
                    assert ve[0] == EXTi(g, c, t)
                    if t % 2 == 1:
                        j = t // 2
                        vector.wait_ge(s_gpc, GP_P2_0 + g * GP_G + 5 + 2 * c + j)
                        emit_quant_dve(vector, y0s[:, 0:2 * iow], 2 * iow // 16,
                                       2 * iow, qs, vsync)
                        vector.engine_nop().then_inc(s_ve, 1)
                        ve[0] += 1
                        assert ve[0] == CHi(g, c, j)
                if g + 1 < NG:
                    emit_div(g + 1)
                assert ve[0] == VEs[("dev", g)]
                for t in range(4):
                    vector.wait_ge(s_mm, MMs[("down", g)] + NC22)
                    vinc(vector.scalar_tensor_tensor(
                        y1b[:, t * D:(t + 1) * D], P[t][:], 0.0, xf[:, t * D:(t + 1) * D],
                        ALU.add, ALU.add, accum_out=lnc[:, 16 + t:17 + t]))
                    vector.scalar_tensor_tensor(
                        sqs[:], y1b[:, t * D:(t + 1) * D], 0.0, y1b[:, t * D:(t + 1) * D],
                        ALU.add, ALU.mult, accum_out=lnc[:, 20 + t:21 + t])
                assert ve[0] == VEs[("ln2", g)]
                vsync(vector.tensor_scalar_mul(lnc[:, 24:28], lnc[:, 16:20], 1.0 / D))
                vsync(vector.tensor_tensor(lnc[:, 28:32], lnc[:, 24:28], lnc[:, 24:28], ALU.mult))
                vsync(vector.scalar_tensor_tensor(lnc[:, 20:24], lnc[:, 20:24], 1.0 / D,
                                                  lnc[:, 28:32], ALU.mult, ALU.subtract))
                vinc(vector.tensor_scalar_add(lnc[:, 20:24], lnc[:, 20:24], EPS_LN))
                vector.wait_ge(s_gpc, GP_P2_0 + g * GP_G + GP_G)
                vinc(vector.scalar_tensor_tensor(lnc[:, 24:28], lnc[:, 24:28], -1.0,
                                                 lnc[:, 28:32], ALU.mult, ALU.mult))
            assert ve[0] == VE_END

        @block.scalar
        def _(scalar):
            sc = [0]
            def sinc(inst):
                inst.then_inc(s_sc, 1)
                sc[0] += 1
            for i in range(NT):
                scalar.wait_ge(s_mmp, i + 1)
                scalar.wait_ge(s_ve, i)
                sinc(scalar.activation(e_sb[:], P[0][:], AF.Exp, bias=0.0, scale=1.0))
                sinc(scalar.activation(r_sb[:], P[0][:], AF.Relu, bias=0.0, scale=1.0))
                scalar.wait_ge(s_mmk, i)
                sinc(scalar.activation(v_sb[:], P[1][:], AF.Copy, bias=0.0, scale=1.0))
            assert sc[0] == SC_P1
            scalar.wait_ge(s_ve, 33)
            scalar.dma_start(out=kv_in.rearrange("(p n) -> p n", p=128),
                             in_=kvs[:]).then_inc(s_kvd, 16)

            def emit_qer(gq):
                assert sc[0] == SCs[("qer", gq)]
                eb = e_sb if gq == 0 else e2
                rb = r_sb if gq == 0 else r2
                for co in range(8):
                    scalar.wait_ge(s_mm, MMs[("q", gq)] + co + 1)
                    if co >= 2:
                        scalar.wait_ge(s_ve, VEs[("comb", gq)] + co - 1)
                    half = (co % 2) * 512
                    src = P[2 + (co % 4) // 2][:, (co % 2) * 512:(co % 2) * 512 + 512]
                    sinc(scalar.activation(eb[:, half:half + 512], src, AF.Exp,
                                           bias=bq_sb[:, co:co + 1], scale=1.0))
                    sinc(scalar.activation(rb[:, half:half + 512], src, AF.Relu,
                                           bias=bq_sb[:, co:co + 1], scale=1.0))

            def emit_numden(gq):
                assert sc[0] == SCs[("nd", gq)]
                for p in range(NPAIR):
                    scalar.wait_ge(s_mm, MMs[("nd", gq)] + p + 1)
                    half = (p % 2) * 512
                    sinc(scalar.activation(xT0[:, p * 512:(p + 1) * 512],
                                           P[2][:, half:half + 512], AF.Copy,
                                           bias=0.0, scale=1.0))
                    sinc(scalar.activation(q_elu[:, p * 512:(p + 1) * 512],
                                           P[3][:, half:half + 512], AF.Copy,
                                           bias=0.0, scale=1.0))

            def emit_xf(gq):
                assert sc[0] == SCs[("xf", gq)]
                for t in range(4):
                    scalar.wait_ge(s_ve, VEs[("ln1", gq)] + 2)
                    sinc(scalar.activation(xf[:, t * D:(t + 1) * D],
                                           y1b[:, t * D:(t + 1) * D], AF.Identity,
                                           bias=lnc[:, 8 + t:9 + t],
                                           scale=lnc[:, 12 + t:13 + t]))

            def emit_silu(gq):
                assert sc[0] == SCs[("silu", gq)]
                for idx in range(NIO * 4):
                    c, t = idx // 4, idx % 4
                    io0, iow = IO_CHUNKS[c]
                    half = (idx % 2) * 512
                    pt = P[2 + idx % 2]
                    scalar.wait_ge(s_mm, MMs[("gu", gq)] + idx + 1)
                    if t >= 2:
                        scalar.wait_ge(s_ve, EXTi(gq, c, t - 2))
                    elif c >= 1:
                        scalar.wait_ge(s_ve, EXTi(gq, c - 1, t + 2))
                    sinc(scalar.activation(e_sb[:, half:half + iow], pt[:, 0:iow],
                                           AF.Silu, bias=0.0, scale=1.0))

            def emit_outs(gq):
                assert sc[0] == SCs[("outs", gq)]
                for t in range(4):
                    scalar.wait_ge(s_ve, VEs[("ln2", gq)] + 2)
                    i = gq * 4 + t
                    sinc(scalar.activation(y1b[:, t * D:(t + 1) * D],
                                           y1b[:, t * D:(t + 1) * D], AF.Identity,
                                           bias=lnc[:, 24 + t:25 + t],
                                           scale=lnc[:, 28 + t:29 + t]))
                    scalar.dma_start(out=out_d[i * 128:(i + 1) * 128, :],
                                     in_=y1b[:, t * D:(t + 1) * D]).then_inc(s_st, 16)

            emit_qer(0)
            emit_numden(0)
            emit_xf(0)
            for g in range(NG):
                if g + 1 < NG:
                    emit_qer(g + 1)
                    emit_numden(g + 1)
                emit_silu(g)
                for c in range(1, NC22, 2):
                    if c == 1:
                        if g == 0:
                            scalar.wait_ge(s_pt, 18)
                        else:
                            scalar.wait_ge(s_mm, MMs[("down", g - 1)] + 22)
                    else:
                        scalar.wait_ge(s_mm, MMs[("down", g)] + (c - 2) + 1)
                    scalar.dma_start(out=wdt[:, D:2 * D],
                                     in_=wdT_d[c * 128:(c + 1) * 128, :]
                                     ).then_inc(s_wd, 16)
                emit_outs(g)
                if g + 1 < NG:
                    emit_xf(g + 1)
            assert sc[0] == SC_END
            scalar.wait_ge(s_st, 16 * NT)

    return nc


_NC_CACHE = None


def kernel(L0_attn_x, L0_attn_Wq, L0_attn_bq, L0_attn_Wk, L0_attn_bk,
           L0_attn_Wv, L0_attn_bv, L0_attn_Wo, L0_attn_bo,
           L0_ffn_w_gate, L0_ffn_w_up, L0_ffn_w_down):
    global _NC_CACHE
    x = np.ascontiguousarray(np.asarray(L0_attn_x))
    B, T, C = x.shape
    assert (B, T, C) == (2, 16384, 1024)
    if _NC_CACHE is None:
        _NC_CACHE = build_nc()
    nc = _NC_CACHE
    in_maps = []
    for core in range(8):
        s, q = core // 4, core % 4
        shard = np.ascontiguousarray(x[s, q * T_CORE:(q + 1) * T_CORE, :])
        in_maps.append({
            "x": shard,
            "Wq": np.asarray(L0_attn_Wq), "Wk": np.asarray(L0_attn_Wk),
            "Wv": np.asarray(L0_attn_Wv), "Wo": np.asarray(L0_attn_Wo),
            "bq": np.asarray(L0_attn_bq), "bk": np.asarray(L0_attn_bk),
            "bv": np.asarray(L0_attn_bv), "bo": np.asarray(L0_attn_bo),
            "wg_sh": np.ascontiguousarray(np.asarray(L0_ffn_w_gate)[core * 352:(core + 1) * 352, :]),
            "wu_sh": np.ascontiguousarray(np.asarray(L0_ffn_w_up)[core * 352:(core + 1) * 352, :]),
            "wd_sh": np.ascontiguousarray(np.asarray(L0_ffn_w_down)[core * 128:(core + 1) * 128, :]),
        })
    res = run_bass_kernel_spmd(nc, in_maps, core_ids=list(range(8)))
    out = np.empty((2, 16384, 1024), dtype=np.float32)
    for core in range(8):
        s, q = core // 4, core % 4
        out[s, q * T_CORE:(q + 1) * T_CORE, :] = res.results[core]["out"]
    return out


# version 19
# speedup vs baseline: 1.0621x; 1.0098x over previous
import contextlib
import numpy as np
import concourse.bass as bass
import concourse.mybir as mybir
from concourse.bass_utils import run_bass_kernel_spmd

F32 = mybir.dt.float32
BF16 = mybir.dt.bfloat16
AF = mybir.ActivationFunctionType
ALU = mybir.AluOpType
AX = mybir.AxisListType

T_CORE = 4096
D = 1024
I_FFN = 2816
NT = 32
NG = 8
NPAIR = 8
EPS_LN = 1e-5
IO_CHUNKS = [(0, 512), (512, 512), (1024, 512), (1536, 512), (2048, 512), (2560, 256)]
NIO = 6
NC22 = 22

SC_P1 = 3 * NT
VE_P2_0 = 34
GP_P2_0 = 17
GP_G = 18

_MM_SIZES = {"q": 8, "nd": 8, "o": 4, "gu": 24, "down": 22}
_MM_ORDER = [("q", 0), ("nd", 0)]
for _g in range(NG):
    _MM_ORDER += [("o", _g)]
    if _g + 1 < NG:
        _MM_ORDER += [("q", _g + 1), ("nd", _g + 1)]
    _MM_ORDER += [("gu", _g), ("down", _g)]
MMs = {}
_c = 0
for _k in _MM_ORDER:
    MMs[_k] = _c
    _c += _MM_SIZES[_k[0]]
assert _c == NG * 66

_SC_SIZES = {"qer": 16, "nd": 16, "outs": 4, "xf": 4, "silu": 24}
_SC_ORDER = [("qer", 0), ("nd", 0), ("xf", 0)]
for _g in range(NG):
    if _g + 1 < NG:
        _SC_ORDER += [("qer", _g + 1), ("nd", _g + 1)]
    _SC_ORDER += [("silu", _g), ("outs", _g)]
    if _g + 1 < NG:
        _SC_ORDER += [("xf", _g + 1)]
SCs = {}
_c = SC_P1
for _k in _SC_ORDER:
    SCs[_k] = _c
    _c += _SC_SIZES[_k[0]]
SC_END = _c

_VE_SIZES = {"comb": 8, "div": 8, "y0": 4, "ln1": 2, "xq": 4, "ext": 36,
             "dev": 4, "ln2": 2}
_VE_ORDER = [("comb", 0), ("div", 0)]
for _g in range(NG):
    _VE_ORDER += [("y0", _g), ("ln1", _g), ("xq", _g)]
    if _g + 1 < NG:
        _VE_ORDER += [("comb", _g + 1)]
    _VE_ORDER += [("ext", _g)]
    if _g + 1 < NG:
        _VE_ORDER += [("div", _g + 1)]
    _VE_ORDER += [("dev", _g), ("ln2", _g)]
VEs = {}
_c = VE_P2_0
for _k in _VE_ORDER:
    VEs[_k] = _c
    _c += _VE_SIZES[_k[0]]
VE_END = _c

def EXTi(g, c, t):
    return VEs[("ext", g)] + 6 * c + (t + 1 if t < 2 else t + 2)
def CHi(g, c, j):
    return VEs[("ext", g)] + 6 * c + (3 if j == 0 else 6)

GM_PRO = 1
def GM_SHLD(k): return 2 + k
def GM_XQ(g): return 10 + 7 * g
def GM_HQC(g, c): return 11 + 7 * g + c

PT_SPECS = ([("g", c) for c in range(8)] + [("u", c) for c in range(8)]
            + [("d", c) for c in range(NC22)])
PT_CC = {"g": 1, "u": 2, "d": 3}
NPT = len(PT_SPECS)

SHARD_TILES = [
    ("g", 0, 128, D), ("g", 128, 128, D), ("g", 256, 96, D),
    ("u", 0, 128, D), ("u", 128, 128, D), ("u", 256, 96, D),
    ("d", 0, 128, 1408), ("d", 1408, 128, 1408),
]
SHARD_CHAINS = [
    (0, 128, 0, D, ("g", 0)), (1, 128, 0, D, ("g", 128)), (2, 96, 0, D, ("g", 256)),
    (3, 128, 0, D, ("u", 0)), (4, 128, 0, D, ("u", 128)), (5, 96, 0, D, ("u", 256)),
    (6, 128, 0, 1024, None), (6, 128, 1024, 384, ("dcol", 0)),
    (7, 128, 0, 1024, None), (7, 128, 1024, 384, ("dcol", 1408)),
]
NCH = 10


def emit_quant_dve(vector, x_ap, nblk, F, qs, vsync):
    P = x_ap.shape[0]
    bm = qs["bm"][0:P, 0:nblk]
    sc = qs["sc"][0:P, 0:nblk]
    rcp = qs["rcp"][0:P, 0:nblk]
    y = qs["y"][0:P, 0:F]
    c2 = qs["c2"][0:P, 0:F]
    w4 = qs["w4"][0:P, 0:F]
    t16 = qs["t16"][0:P, 0:F]
    r4 = qs["r4"][0:P, 0:F]
    d2 = qs["d2"][0:P, 0:F]
    vsync(vector.tensor_reduce(bm, x_ap.rearrange("p (b k) -> p b k", k=16), AX.X,
                               ALU.max, apply_absolute_value=True))
    vsync(vector.tensor_scalar(sc, bm, 1.0 / 6.0, 1e-12, ALU.mult, ALU.max))
    vsync(vector.reciprocal(rcp, sc))
    vsync(vector.tensor_scalar_mul(sc, sc, 0.25))
    rcp_rep = rcp.unsqueeze(2).broadcast_to([P, nblk, 16])
    vector.tensor_tensor(y.rearrange("p (b k) -> p b k", k=16),
                         x_ap.rearrange("p (b k) -> p b k", k=16), rcp_rep, ALU.mult)
    vector.tensor_scalar(c2, y, 2.0, -2.0, ALU.min, ALU.max)
    vector.scalar_tensor_tensor(w4, c2, 2.0, y, ALU.mult, ALU.add)
    vector.tensor_scalar(c2, y, 4.0, -4.0, ALU.min, ALU.max)
    vector.tensor_tensor(w4, w4, c2, ALU.add)
    vector.tensor_scalar_mul(t16, w4, 0.5)
    vector.tensor_scalar_mul(r4, t16, 2.0)
    vector.tensor_scalar(d2, r4, 8.0, -8.0, ALU.min, ALU.max)
    vector.scalar_tensor_tensor(d2, r4, 4.0, d2, ALU.mult, ALU.subtract)
    vector.tensor_scalar(r4, r4, 12.0, -12.0, ALU.min, ALU.max)
    vector.scalar_tensor_tensor(r4, r4, -2.0, d2, ALU.mult, ALU.add)


def emit_quant_gp_ret(gpsimd, q_out, nblk, F, qs):
    Pn = q_out.shape[0]
    sc_rep = qs["sc"][0:Pn, 0:nblk].unsqueeze(2).broadcast_to([Pn, nblk, 16])
    return gpsimd.tensor_tensor(q_out.rearrange("p (b k) -> p b k", k=16),
                                qs["r4"][0:Pn, 0:F].rearrange("p (b k) -> p b k", k=16),
                                sc_rep, ALU.mult)


def build_nc():
    nc = bass.Bass("TRN2", target_bir_lowering=False, debug=False,
                   detect_race_conditions=False)
    x_d = nc.dram_tensor("x", [T_CORE, D], F32, kind="ExternalInput").ap()
    w_in = {k: nc.dram_tensor("W" + k, [D, D], F32, kind="ExternalInput").ap()
            for k in ("q", "k", "v", "o")}
    bq_d = nc.dram_tensor("bq", [D], F32, kind="ExternalInput").ap()
    bk_d = nc.dram_tensor("bk", [D], F32, kind="ExternalInput").ap()
    bv_d = nc.dram_tensor("bv", [D], F32, kind="ExternalInput").ap()
    bo_d = nc.dram_tensor("bo", [D], F32, kind="ExternalInput").ap()
    wg_sh_d = nc.dram_tensor("wg_sh", [I_FFN // 8, D], F32, kind="ExternalInput").ap()
    wu_sh_d = nc.dram_tensor("wu_sh", [I_FFN // 8, D], F32, kind="ExternalInput").ap()
    wd_sh_d = nc.dram_tensor("wd_sh", [D // 8, I_FFN], F32, kind="ExternalInput").ap()
    out_d = nc.dram_tensor("out", [T_CORE, D], F32, kind="ExternalOutput").ap()

    xbf_d = nc.dram_tensor("xbf_d", [T_CORE, D], BF16, kind="Internal").ap()
    wbf = {k: nc.dram_tensor("wbf_" + k, [D, D], BF16, kind="Internal").ap()
           for k in ("q", "k", "v", "o")}
    qin = {m: nc.dram_tensor("q%s_in" % m, [I_FFN // 8 * D if m != "d" else D // 8 * I_FFN],
                             BF16, kind="Internal").ap() for m in ("g", "u", "d")}
    qout = {m: nc.dram_tensor("q%s_out" % m, [I_FFN * D], BF16, kind="Internal").ap()
            for m in ("g", "u", "d")}
    wgT_d = nc.dram_tensor("wgT_d", [D, I_FFN], BF16, kind="Internal").ap()
    wuT_d = nc.dram_tensor("wuT_d", [D, I_FFN], BF16, kind="Internal").ap()
    wdT_d = nc.dram_tensor("wdT_d", [I_FFN, D], BF16, kind="Internal").ap()
    kv_in = nc.dram_tensor("kv_in", [128 * 1032], F32, kind="Internal").ap()
    kv_out = nc.dram_tensor("kv_out", [128 * 1032], F32, kind="Internal").ap()
    xq_d = nc.dram_tensor("xq_d", [512, D], BF16, kind="Internal").ap()
    hq_d = nc.dram_tensor("hq_d", [512, I_FFN], BF16, kind="Internal").ap()

    LM = {}
    L_PT = []

    with contextlib.ExitStack() as stack:
        E = stack.enter_context
        wA = E(nc.sbuf_tensor("wA", [128, 8 * D], BF16))
        wB = E(nc.sbuf_tensor("wB", [128, 8 * D], BF16))
        xT0 = E(nc.sbuf_tensor("xT0", [128, 8 * 512], BF16))
        xT1 = E(nc.sbuf_tensor("xT1", [128, 8 * 512], BF16))
        k_sb = E(nc.sbuf_tensor("k_sb", [128, D], BF16))
        v_sb = E(nc.sbuf_tensor("v_sb", [128, D], BF16))
        e_sb = E(nc.sbuf_tensor("e_sb", [128, D], F32))
        r_sb = E(nc.sbuf_tensor("r_sb", [128, D], F32))
        kvs = E(nc.sbuf_tensor("kvs", [128, 1032], F32))
        kvbd = E(nc.sbuf_tensor("kvbd", [128, NPAIR * 128], BF16))
        ksr = E(nc.sbuf_tensor("ksr", [128, NPAIR * 128], BF16))
        q_elu = E(nc.sbuf_tensor("q_elu", [128, 8 * 512], BF16))
        xf = E(nc.sbuf_tensor("xf", [128, 4 * D], F32))
        y1b = E(nc.sbuf_tensor("y1b", [128, 4 * D], F32))
        y0s = E(nc.sbuf_tensor("y0s", [128, D], F32))
        sqs = E(nc.sbuf_tensor("sqs", [128, D], F32))
        hq_sb = E(nc.sbuf_tensor("hq_sb", [128, 4 * I_FFN], BF16))
        wgu = E(nc.sbuf_tensor("wgu", [128, 2 * 2 * 8 * 512], BF16))
        wdt = E(nc.sbuf_tensor("wdt", [128, 2 * D], BF16))
        qs = {
            "bm": E(nc.sbuf_tensor("q_bm", [128, 64], F32)),
            "sc": E(nc.sbuf_tensor("q_sc", [128, 64], F32)),
            "rcp": E(nc.sbuf_tensor("q_rcp", [128, 64], F32)),
            "y": E(nc.sbuf_tensor("q_y", [128, 1024], F32)),
            "c2": E(nc.sbuf_tensor("q_c2", [128, 1024], F32)),
            "w4": E(nc.sbuf_tensor("q_w4", [128, 1024], F32)),
            "t16": E(nc.sbuf_tensor("q_t16", [128, 1024], mybir.dt.int16)),
            "r4": E(nc.sbuf_tensor("q_r4", [128, 1024], BF16)),
            "d2": E(nc.sbuf_tensor("q_d2", [128, 1024], BF16)),
        }
        shw = E(nc.sbuf_tensor("shw", [128, 1408], F32))
        shq = E(nc.sbuf_tensor("shq", [128, 1408], BF16))
        lnc = E(nc.sbuf_tensor("lnc", [128, 32], F32))
        ones1 = E(nc.sbuf_tensor("ones1", [1, 128], BF16))
        onecol = E(nc.sbuf_tensor("onecol", [128, 1], BF16))
        mhalf = E(nc.sbuf_tensor("mhalf", [128, 1], F32))
        brows = E(nc.sbuf_tensor("brows", [1, 3 * D], BF16))
        bq_sb = E(nc.sbuf_tensor("bq_sb", [128, 8], F32))
        P = [E(nc.psum_tensor("P%d" % i, [128, 1024], F32)) for i in range(4)]

        e2 = kvs[:, 0:1024]
        r2 = shw[:, 0:1024]

        s_ld = E(nc.semaphore("s_ld"))
        s_gpd = E(nc.semaphore("s_gpd"))
        s_gpc = E(nc.semaphore("s_gpc"))
        s_cc = E(nc.semaphore("s_cc"))
        s_mmp = E(nc.semaphore("s_mmp"))
        s_mmk = E(nc.semaphore("s_mmk"))
        s_mm = E(nc.semaphore("s_mm"))
        s_ve = E(nc.semaphore("s_ve"))
        s_vq = E(nc.semaphore("s_vq"))
        s_sc = E(nc.semaphore("s_sc"))
        s_st = E(nc.semaphore("s_st"))
        s_gm = E(nc.semaphore("s_gm"))
        s_lm = E(nc.semaphore("s_lm"))
        s_vs = E(nc.semaphore("s_vs"))
        s_pt = E(nc.semaphore("s_pt"))
        s_kvd = E(nc.semaphore("s_kvd"))
        s_wd = E(nc.semaphore("s_wd"))
        WD = {(g, c): 16 * (11 * g + (c - 1) // 2 + 1)
              for g in range(NG) for c in range(1, NC22, 2)}

        block = E(nc.Block())

        @block.sync
        def _(sync):
            ld = 0
            lm = 0
            def dma(out, in_, transpose=False, slow=False):
                nonlocal ld
                sync.dma_start(out=out, in_=in_, transpose=transpose,
                               allow_slow_non_contiguous=slow).then_inc(s_ld, 16)
                ld += 16
            def milestone(key):
                nonlocal lm
                sync.wait_ge(s_ld, ld)
                sync.sem_inc(s_lm, 1)
                lm += 1
                LM[key] = lm
            def load_xT0(g):
                for c in range(8):
                    dma(xT0[:, c * 512:(c + 1) * 512],
                        xbf_d[g * 512:(g + 1) * 512, c * 128:(c + 1) * 128],
                        transpose=True)
                milestone(("xT0", g))
            def load_xf(g):
                for t in range(4):
                    dma(xf[:, t * D:(t + 1) * D],
                        x_d[g * 512 + t * 128: g * 512 + (t + 1) * 128, :])
                milestone(("xf", g))
            dma(bq_sb[:], bq_d.rearrange("(c p) -> p c", p=128), slow=True)
            sync.wait_ge(s_gm, GM_PRO)
            for name, dst in (("k", wA), ("v", wB)):
                for c in range(8):
                    dma(dst[:, c * D:(c + 1) * D],
                        wbf[name][:, c * 128:(c + 1) * 128], transpose=True)
            milestone("wkv")
            for g in range(NG):
                if g >= 2:
                    sync.wait_ge(s_mmp, 4 * (g - 1))
                buf = xT0 if g % 2 == 0 else xT1
                for c in range(8):
                    dma(buf[:, c * 512:(c + 1) * 512],
                        xbf_d[g * 512:(g + 1) * 512, c * 128:(c + 1) * 128],
                        transpose=True)
                milestone(("p1x", g))
            for j, (m, c) in enumerate(PT_SPECS):
                sync.wait_ge(s_cc, PT_CC[m])
                if j >= 4:
                    sync.wait_ge(s_pt, j - 3)
                rows = I_FFN if m in ("g", "u") else D
                cols = D if m in ("g", "u") else I_FFN
                srcm = qout[m][0:I_FFN * D].rearrange("(r c) -> r c", c=cols)
                dma(hq_sb[:, (j % 4) * I_FFN:(j % 4) * I_FFN + rows],
                    srcm[:, c * 128:(c + 1) * 128], transpose=True)
                L_PT.append(ld)
            sync.wait_ge(s_mmk, NT)
            for name, dst in (("q", wA), ("o", wB)):
                for c in range(8):
                    dma(dst[:, c * D:(c + 1) * D],
                        wbf[name][:, c * 128:(c + 1) * 128], transpose=True)
            milestone("wqo")
            load_xT0(0)
            load_xf(0)
            sync.wait_ge(s_cc, 4)
            dma(kvs[:], kv_out.rearrange("(p n) -> p n", p=128))
            milestone("kvrel")
            for g in range(NG):
                for c in range(2):
                    if g == 0:
                        sync.wait_ge(s_pt, 16)
                    else:
                        sync.wait_ge(s_mm, MMs[("gu", g - 1)] + (20 if c == 0 else 24))
                    io0, iow = IO_CHUNKS[c]
                    h0 = (c % 2) * 8192
                    dma(wgu[:, h0:h0 + 4096].rearrange("p (a w) -> p a w", w=512)[:, :, 0:iow],
                        wgT_d.rearrange("(a p) io -> p a io", p=128)[:, :, io0:io0 + iow])
                    dma(wgu[:, h0 + 4096:h0 + 8192].rearrange("p (a w) -> p a w", w=512)[:, :, 0:iow],
                        wuT_d.rearrange("(a p) io -> p a io", p=128)[:, :, io0:io0 + iow])
                    milestone(("guc", g, c))
                if g + 1 < NG:
                    sync.wait_ge(s_ve, VEs[("div", g)] + 8)
                    load_xT0(g + 1)
                if g == 0:
                    sync.wait_ge(s_pt, 17)
                else:
                    sync.wait_ge(s_mm, MMs[("down", g - 1)] + 21)
                dma(wdt[:, 0:D], wdT_d[0:128, :])
                milestone(("dn", g, 0))
                if g >= 1:
                    sync.wait_ge(s_ve, VEs[("ln2", g - 1)] + 2)
                    load_xf(g)
                sync.wait_ge(s_gm, GM_XQ(g))
                for c in range(8):
                    dma(xT1[:, c * 512:(c + 1) * 512],
                        xq_d[:, c * 128:(c + 1) * 128], transpose=True)
                milestone(("xqT", g))
                for c in range(2, NIO):
                    io0, iow = IO_CHUNKS[c]
                    h0 = (c % 2) * 8192
                    sync.wait_ge(s_mm, MMs[("gu", g)] + 4 * (c - 1))
                    dma(wgu[:, h0:h0 + 4096].rearrange("p (a w) -> p a w", w=512)[:, :, 0:iow],
                        wgT_d.rearrange("(a p) io -> p a io", p=128)[:, :, io0:io0 + iow])
                    dma(wgu[:, h0 + 4096:h0 + 8192].rearrange("p (a w) -> p a w", w=512)[:, :, 0:iow],
                        wuT_d.rearrange("(a p) io -> p a io", p=128)[:, :, io0:io0 + iow])
                    milestone(("guc", g, c))
                nco = 0
                for c in range(NIO):
                    io0, iow = IO_CHUNKS[c]
                    sync.wait_ge(s_gm, GM_HQC(g, c))
                    for cc in range(io0 // 128, (io0 + iow) // 128):
                        dma(hq_sb[:, cc * 512:(cc + 1) * 512],
                            hq_d[:, cc * 128:(cc + 1) * 128], transpose=True)
                        nco += 1
                    milestone(("hqTc", g, c))
                assert nco == NC22
                for c in range(2, NC22, 2):
                    sync.wait_ge(s_mm, MMs[("down", g)] + (c - 2) + 1)
                    dma(wdt[:, 0:D], wdT_d[c * 128:(c + 1) * 128, :])
                    milestone(("dnc", g, c))

        @block.gpsimd
        def _(gpsimd):
            gpd = 0
            gpc = 0
            gm = 0
            def gdma(out, in_):
                nonlocal gpd
                gpsimd.dma_start(out=out, in_=in_).then_inc(s_gpd, 16)
                gpd += 16
            def ginc(inst):
                nonlocal gpc
                inst.then_inc(s_gpc, 1)
                gpc += 1
            def gmilestone(expect=None):
                nonlocal gm
                gpsimd.wait_ge(s_gpd, gpd)
                gpsimd.sem_inc(s_gm, 1)
                gm += 1
                if expect is not None:
                    assert gm == expect, (gm, expect)
            ginc(gpsimd.memset(ones1[:], 1.0))
            ginc(gpsimd.memset(onecol[:], 1.0))
            ginc(gpsimd.memset(mhalf[:], -0.5))
            ginc(gpsimd.memset(kvbd[:], 0.0))
            ginc(gpsimd.memset(ksr[:], 0.0))
            ginc(gpsimd.engine_nop())
            ginc(gpsimd.engine_nop())
            assert gpc == 7
            for j in range(4):
                gdma(xbf_d[j * 1024:(j + 1) * 1024, :], x_d[j * 1024:(j + 1) * 1024, :])
            for name in ("q", "k", "v", "o"):
                gdma(wbf[name], w_in[name])
            for j, b in enumerate((bk_d, bv_d, bo_d)):
                gdma(brows[:, j * D:(j + 1) * D], b[None, :])
            gmilestone(GM_PRO)
            def shard_load(k):
                m, r0, nr, w = SHARD_TILES[k]
                if m == "d":
                    gdma(shw[0:128, 0:w], wd_sh_d[:, r0:r0 + w])
                else:
                    srct = {"g": wg_sh_d, "u": wu_sh_d}[m]
                    gdma(shw[0:nr, 0:w], srct[r0:r0 + nr, :])
            shard_load(0)
            gmilestone(GM_SHLD(0))
            for j, (li, nr, c0, w, wr) in enumerate(SHARD_CHAINS):
                gpsimd.wait_ge(s_vq, j + 1)
                if j + 1 < len(SHARD_CHAINS) and SHARD_CHAINS[j + 1][0] == li + 1:
                    shard_load(li + 1)
                    gmilestone(GM_SHLD(li + 1))
                gpsimd.wait_ge(s_gpd, gpd)
                ginc(emit_quant_gp_ret(gpsimd, shq[0:nr, c0:c0 + w], w // 16, w, qs))
                if wr is not None:
                    kind, base = wr
                    if kind == "dcol":
                        gdma(qin["d"][0:128 * I_FFN].rearrange("(r c) -> r c", c=I_FFN)[:, base:base + 1408],
                             shq[0:128, 0:1408])
                    else:
                        gdma(qin[kind][base * D:(base + nr) * D].rearrange("(r c) -> r c", c=D),
                             shq[0:nr, 0:D])
            gpsimd.wait_ge(s_gpd, gpd)
            assert gpc == GP_P2_0
            gpsimd.collective_compute("AllGather", ALU.bypass,
                                      replica_groups=[[0, 1, 2, 3, 4, 5, 6, 7]],
                                      ins=[qin["g"]], outs=[qout["g"]]).then_inc(s_cc, 1)
            gpsimd.collective_compute("AllGather", ALU.bypass,
                                      replica_groups=[[0, 1, 2, 3, 4, 5, 6, 7]],
                                      ins=[qin["u"]], outs=[qout["u"]]).then_inc(s_cc, 1)
            gpsimd.collective_compute("AllGather", ALU.bypass,
                                      replica_groups=[[0, 1, 2, 3, 4, 5, 6, 7]],
                                      ins=[qin["d"]], outs=[qout["d"]]).then_inc(s_cc, 1)
            gpd_after = []
            for j, (m, c) in enumerate(PT_SPECS):
                gpsimd.wait_ge(s_ld, L_PT[j])
                rows = I_FFN if m in ("g", "u") else D
                dstT = {"g": wgT_d, "u": wuT_d, "d": wdT_d}[m]
                gdma(dstT[c * 128:(c + 1) * 128, :],
                     hq_sb[:, (j % 4) * I_FFN:(j % 4) * I_FFN + rows])
                gpd_after.append(gpd)
                if j >= 1:
                    gpsimd.wait_ge(s_gpd, gpd_after[j - 1])
                    gpsimd.sem_inc(s_pt, 1)
            gpsimd.wait_ge(s_gpd, gpd_after[-1])
            gpsimd.sem_inc(s_pt, 1)
            gpsimd.wait_ge(s_kvd, 16)
            gpsimd.collective_compute("AllReduce", ALU.add,
                                      replica_groups=[[0, 1, 2, 3], [4, 5, 6, 7]],
                                      ins=[kv_in], outs=[kv_out]).then_inc(s_cc, 1)
            for g in range(NG):
                base_gp = GP_P2_0 + g * GP_G
                gpsimd.wait_ge(s_ve, VEs[("ln1", g)] + 1)
                ginc(gpsimd.tensor_tensor(lnc[:, 12:16], lnc[:, 4:8],
                                          mhalf[:].broadcast_to([128, 4]), ALU.pow))
                assert gpc == base_gp + 1
                for t in range(4):
                    gpsimd.wait_ge(s_ve, VEs[("xq", g)] + t + 1)
                    ginc(emit_quant_gp_ret(gpsimd, xT1[:, t * D:(t + 1) * D], 64, D, qs))
                assert gpc == base_gp + 5
                gdma(xq_d.rearrange("(t p) c -> p t c", p=128),
                     xT1[:].rearrange("p (t c) -> p t c", c=D))
                gmilestone(GM_XQ(g))
                for c in range(NIO):
                    io0, iow = IO_CHUNKS[c]
                    for j in range(2):
                        gpsimd.wait_ge(s_ve, CHi(g, c, j))
                        blk = 4 * io0 + 2 * j * iow
                        ginc(emit_quant_gp_ret(
                            gpsimd, hq_sb[:, blk:blk + 2 * iow], 2 * iow // 16,
                            2 * iow, qs))
                    gdma(hq_d.rearrange("(t p) z -> p t z", p=128)[:, :, io0:io0 + iow],
                         hq_sb[:, 4 * io0:4 * (io0 + iow)].rearrange(
                             "p (t w) -> p t w", w=iow))
                    gmilestone(GM_HQC(g, c))
                gpsimd.wait_ge(s_ve, VEs[("ln2", g)] + 1)
                ginc(gpsimd.tensor_tensor(lnc[:, 28:32], lnc[:, 20:24],
                                          mhalf[:].broadcast_to([128, 4]), ALU.pow))
                assert gpc == base_gp + GP_G

        @block.tensor
        def _(tensor):
            mmp = mmk = 0
            mm = [0]
            def minc(inst):
                inst.then_inc(s_mm, 1)
                mm[0] += 1
            for i in range(NT + 1):
                if i < NT:
                    g, tl = i // 4, i % 4
                    if i == 0:
                        tensor.wait_ge(s_gpc, 7)
                        tensor.wait_ge(s_gm, GM_PRO)
                        tensor.wait_ge(s_lm, LM["wkv"])
                    if tl == 0:
                        tensor.wait_ge(s_lm, LM[("p1x", g)])
                    tensor.wait_ge(s_sc, 3 * i)
                    xbuf = xT0 if g % 2 == 0 else xT1
                    for dst, wT, bias_off in ((P[0], wA, 0), (P[1], wB, D)):
                        for h in range(2):
                            for cf in range(8):
                                tensor.matmul(
                                    dst[:, h * 512:(h + 1) * 512],
                                    xbuf[:, cf * 512 + tl * 128: cf * 512 + (tl + 1) * 128],
                                    wT[:, cf * D + h * 512: cf * D + (h + 1) * 512],
                                    start=(cf == 0), stop=False)
                            last = tensor.matmul(dst[:, h * 512:(h + 1) * 512], ones1[:],
                                          brows[:, bias_off + h * 512: bias_off + (h + 1) * 512],
                                          start=False, stop=True)
                    last.then_inc(s_mmp, 1)
                    mmp += 1
                if i >= 1:
                    j = i - 1
                    tensor.wait_ge(s_ve, j + 1)
                    if i == NT:
                        tensor.wait_ge(s_sc, 3 * NT)
                    for p in range(NPAIR):
                        tensor.matmul(P[2][:, p * 128:(p + 1) * 128],
                                      k_sb[:, p * 128:(p + 1) * 128],
                                      v_sb[:, p * 128:(p + 1) * 128],
                                      start=(j == 0 and p % 4 == 0),
                                      stop=(j == NT - 1 and p % 4 == 3))
                        last = tensor.matmul(P[3][:, p:p + 1],
                                      k_sb[:, p * 128:(p + 1) * 128],
                                      onecol[:],
                                      start=(j == 0 and p == 0),
                                      stop=(j == NT - 1 and p == 7))
                    last.then_inc(s_mmk, 1)
                    mmk += 1
            assert mmp == NT and mmk == NT

            def emit_q(gq):
                assert mm[0] == MMs[("q", gq)]
                tensor.wait_ge(s_lm, LM[("xT0", gq)])
                for co in range(8):
                    if co == 0:
                        if gq == 0:
                            tensor.wait_ge(s_ve, 33)
                        else:
                            tensor.wait_ge(s_ve, VEs[("y0", gq - 1)] + 4)
                    if co >= 4:
                        tensor.wait_ge(s_sc, SCs[("qer", gq)] + 2 * (co - 3))
                    pdst = P[2 + (co % 4) // 2]
                    for cf in range(8):
                        last = tensor.matmul(pdst[:, (co % 2) * 512:(co % 2) * 512 + 512],
                                      wA[:, cf * D + co * 128: cf * D + (co + 1) * 128],
                                      xT0[:, cf * 512:(cf + 1) * 512],
                                      start=(cf == 0), stop=(cf == 7))
                    minc(last)

            def emit_nd(gq):
                assert mm[0] == MMs[("nd", gq)]
                for p in range(NPAIR):
                    tensor.wait_ge(s_ve, VEs[("comb", gq)] + p + 1)
                    if p == 0:
                        tensor.wait_ge(s_sc, SCs[("qer", gq)] + 14)
                    elif p == 1:
                        tensor.wait_ge(s_sc, SCs[("qer", gq)] + 16)
                    else:
                        tensor.wait_ge(s_sc, SCs[("nd", gq)] + 2 * (p - 1))
                    half = (p % 2) * 512
                    tensor.matmul(P[2][:, half:half + 512],
                                  kvbd[:, p * 128:(p + 1) * 128],
                                  q_elu[:, p * 512:(p + 1) * 512],
                                  start=True, stop=True)
                    last = tensor.matmul(P[3][:, half:half + 512],
                                  ksr[:, p * 128:(p + 1) * 128],
                                  q_elu[:, p * 512:(p + 1) * 512],
                                  start=True, stop=True)
                    minc(last)

            emit_q(0)
            emit_nd(0)
            for g in range(NG):
                assert mm[0] == MMs[("o", g)]
                for t in range(4):
                    if t == 0:
                        if g == 0:
                            tensor.wait_ge(s_ve, VEs[("div", 0)] + 8)
                        else:
                            tensor.wait_ge(s_ve, VEs[("dev", g - 1)] + 3)
                    else:
                        tensor.wait_ge(s_ve, VEs[("y0", g)] + t)
                    for h in range(2):
                        for p in range(NPAIR):
                            tensor.matmul(P[2][:, h * 512:(h + 1) * 512],
                                          q_elu[:, p * 512 + t * 128: p * 512 + (t + 1) * 128],
                                          wB[:, p * D + h * 512: p * D + (h + 1) * 512],
                                          start=(p == 0), stop=False)
                        last = tensor.matmul(P[2][:, h * 512:(h + 1) * 512], ones1[:],
                                      brows[:, 2 * D + h * 512: 2 * D + (h + 1) * 512],
                                      start=False, stop=True)
                    minc(last)
                if g + 1 < NG:
                    emit_q(g + 1)
                    emit_nd(g + 1)
                assert mm[0] == MMs[("gu", g)]
                tensor.wait_ge(s_lm, LM[("xqT", g)])
                if g + 1 < NG:
                    tensor.wait_ge(s_sc, SCs[("nd", g + 1)] + 16)
                for idx in range(NIO * 4):
                    c, t = idx // 4, idx % 4
                    io0, iow = IO_CHUNKS[c]
                    h0 = (c % 2) * 8192
                    if t == 0:
                        tensor.wait_ge(s_lm, LM[("guc", g, c)])
                    if t >= 2:
                        tensor.wait_ge(s_ve, EXTi(g, c, t - 2))
                    elif c >= 1:
                        tensor.wait_ge(s_ve, EXTi(g, c - 1, t + 2))
                    pt = P[2 + idx % 2]
                    for cf in range(8):
                        tensor.matmul(pt[:, 0:iow],
                                      xT1[:, cf * 512 + t * 128: cf * 512 + (t + 1) * 128],
                                      wgu[:, h0 + cf * 512: h0 + cf * 512 + iow],
                                      start=(cf == 0), stop=(cf == 7))
                    for cf in range(8):
                        last = tensor.matmul(pt[:, 512:512 + iow],
                                      xT1[:, cf * 512 + t * 128: cf * 512 + (t + 1) * 128],
                                      wgu[:, h0 + 4096 + cf * 512: h0 + 4096 + cf * 512 + iow],
                                      start=(cf == 0), stop=(cf == 7))
                    minc(last)
                assert mm[0] == MMs[("down", g)]
                tensor.wait_ge(s_ve, EXTi(g, 5, 3))
                for c in range(NC22):
                    if c in (0, 4, 8, 12, 16, 20):
                        tensor.wait_ge(s_lm, LM[("hqTc", g, c // 4)])
                    if c % 2 == 0:
                        tensor.wait_ge(s_lm, LM[("dn", g, 0) if c == 0 else ("dnc", g, c)])
                    else:
                        tensor.wait_ge(s_wd, WD[(g, c)])
                    for t in range(4):
                        for h in range(2):
                            last = tensor.matmul(P[t][:, h * 512:(h + 1) * 512],
                                          hq_sb[:, c * 512 + t * 128: c * 512 + (t + 1) * 128],
                                          wdt[:, (c % 2) * D + h * 512: (c % 2) * D + (h + 1) * 512],
                                          start=(c == 0), stop=(c == NC22 - 1))
                    minc(last)
            assert mm[0] == NG * 66

        @block.vector
        def _(vector):
            ve = [0]
            vq = 0
            vs_n = [0]
            def vinc(inst):
                inst.then_inc(s_ve, 1)
                ve[0] += 1
            def vsync(inst):
                vs_n[0] += 1
                inst.then_inc(s_vs, 1)
                vector.wait_ge(s_vs, vs_n[0])
            for i in range(NT):
                vector.wait_ge(s_mmk, i)
                vector.wait_ge(s_sc, 3 * i + 2)
                vinc(vector.scalar_tensor_tensor(k_sb[:], e_sb[:], 1.0, r_sb[:],
                                                 ALU.min, ALU.add))
                assert ve[0] == i + 1
                if i < NCH:
                    li, nr, c0, w, _wr = SHARD_CHAINS[i]
                    vector.wait_ge(s_gm, GM_SHLD(li))
                    vector.wait_ge(s_gpc, 7 + i)
                    emit_quant_dve(vector, shw[0:nr, c0:c0 + w], w // 16, w, qs, vsync)
                    vector.engine_nop().then_inc(s_vq, 1)
                    vq += 1
            vector.wait_ge(s_mmk, NT)
            vector.tensor_copy(kvs[:, 0:512], P[2][:, 0:512])
            vector.tensor_copy(kvs[:, 512:1024], P[2][:, 512:1024])
            vinc(vector.tensor_copy(kvs[:, 1024:1032], P[3][:, 0:8]))
            assert ve[0] == 33
            vector.wait_ge(s_lm, LM["kvrel"])
            vector.wait_ge(s_gpc, 7)
            for p in range(NPAIR):
                src0 = kvs[0:64, p * 128: p * 128 + 64]
                src1 = kvs[64:128, p * 128 + 64: p * 128 + 128]
                vector.tensor_copy(kvbd[0:64, p * 128: p * 128 + 64], src0)
                vector.tensor_copy(kvbd[64:128, p * 128 + 64: p * 128 + 128], src1)
                ks = kvs[:, 1024 + p:1025 + p]
                vector.tensor_copy(ksr[0:64, p * 128: p * 128 + 64],
                                   ks[0:64, :].broadcast_to([64, 64]))
                vector.tensor_copy(ksr[64:128, p * 128 + 64: p * 128 + 128],
                                   ks[64:128, :].broadcast_to([64, 64]))
            vinc(vector.engine_nop())
            assert ve[0] == VE_P2_0

            def emit_comb(gq):
                assert ve[0] == VEs[("comb", gq)]
                eb = e_sb if gq == 0 else e2
                rb = r_sb if gq == 0 else r2
                for co in range(8):
                    vector.wait_ge(s_sc, SCs[("qer", gq)] + 2 * (co + 1))
                    half = (co % 2) * 512
                    vinc(vector.scalar_tensor_tensor(
                        q_elu[:, co * 512:(co + 1) * 512],
                        eb[:, half:half + 512], 1.0, rb[:, half:half + 512],
                        ALU.min, ALU.add))

            def emit_div(gq):
                assert ve[0] == VEs[("div", gq)]
                for p in range(NPAIR):
                    vector.wait_ge(s_sc, SCs[("nd", gq)] + 2 * (p + 1))
                    vector.tensor_scalar_max(sqs[:, 0:512],
                                             q_elu[:, p * 512:(p + 1) * 512], 1e-6)
                    vector.reciprocal(sqs[:, 512:1024], sqs[:, 0:512])
                    vinc(vector.tensor_tensor(q_elu[:, p * 512:(p + 1) * 512],
                                              xT0[:, p * 512:(p + 1) * 512],
                                              sqs[:, 512:1024], ALU.mult))

            emit_comb(0)
            emit_div(0)
            for g in range(NG):
                assert ve[0] == VEs[("y0", g)]
                vector.wait_ge(s_lm, LM[("xf", g)])
                for t in range(4):
                    vector.wait_ge(s_mm, MMs[("o", g)] + t + 1)
                    if g >= 1:
                        vector.wait_ge(s_sc, SCs[("outs", g - 1)] + t + 1)
                        vector.wait_ge(s_st, 16 * (4 * (g - 1) + t + 1))
                    vinc(vector.scalar_tensor_tensor(
                        y1b[:, t * D:(t + 1) * D], P[2][:], 0.0, xf[:, t * D:(t + 1) * D],
                        ALU.add, ALU.add, accum_out=lnc[:, t:t + 1]))
                    vector.scalar_tensor_tensor(
                        sqs[:], y1b[:, t * D:(t + 1) * D], 0.0, y1b[:, t * D:(t + 1) * D],
                        ALU.add, ALU.mult, accum_out=lnc[:, 4 + t:5 + t])
                assert ve[0] == VEs[("ln1", g)]
                vsync(vector.tensor_scalar_mul(lnc[:, 8:12], lnc[:, 0:4], 1.0 / D))
                vsync(vector.tensor_tensor(lnc[:, 12:16], lnc[:, 8:12], lnc[:, 8:12], ALU.mult))
                vsync(vector.scalar_tensor_tensor(lnc[:, 4:8], lnc[:, 4:8], 1.0 / D,
                                                  lnc[:, 12:16], ALU.mult, ALU.subtract))
                vinc(vector.tensor_scalar_add(lnc[:, 4:8], lnc[:, 4:8], EPS_LN))
                vector.wait_ge(s_gpc, GP_P2_0 + g * GP_G + 1)
                vinc(vector.scalar_tensor_tensor(lnc[:, 8:12], lnc[:, 8:12], -1.0,
                                                 lnc[:, 12:16], ALU.mult, ALU.mult))
                assert ve[0] == VEs[("xq", g)]
                for t in range(4):
                    vector.wait_ge(s_sc, SCs[("xf", g)] + t + 1)
                    vector.wait_ge(s_gpc, GP_P2_0 + g * GP_G + 1 + t)
                    emit_quant_dve(vector, xf[:, t * D:(t + 1) * D], 64, D, qs, vsync)
                    vinc(vector.engine_nop())
                if g + 1 < NG:
                    emit_comb(g + 1)
                assert ve[0] == VEs[("ext", g)]
                for idx in range(NIO * 4):
                    c, t = idx // 4, idx % 4
                    io0, iow = IO_CHUNKS[c]
                    half = (idx % 2) * 512
                    pt = P[2 + idx % 2]
                    vector.wait_ge(s_sc, SCs[("silu", g)] + idx + 1)
                    vinc(vector.tensor_tensor(y0s[:, (t % 2) * iow:(t % 2 + 1) * iow],
                                              e_sb[:, half:half + iow],
                                              pt[:, 512:512 + iow], ALU.mult))
                    assert ve[0] == EXTi(g, c, t)
                    if t % 2 == 1:
                        j = t // 2
                        vector.wait_ge(s_gpc, GP_P2_0 + g * GP_G + 5 + 2 * c + j)
                        emit_quant_dve(vector, y0s[:, 0:2 * iow], 2 * iow // 16,
                                       2 * iow, qs, vsync)
                        vector.engine_nop().then_inc(s_ve, 1)
                        ve[0] += 1
                        assert ve[0] == CHi(g, c, j)
                if g + 1 < NG:
                    emit_div(g + 1)
                assert ve[0] == VEs[("dev", g)]
                for t in range(4):
                    vector.wait_ge(s_mm, MMs[("down", g)] + NC22)
                    vinc(vector.scalar_tensor_tensor(
                        y1b[:, t * D:(t + 1) * D], P[t][:], 0.0, xf[:, t * D:(t + 1) * D],
                        ALU.add, ALU.add, accum_out=lnc[:, 16 + t:17 + t]))
                    vector.scalar_tensor_tensor(
                        sqs[:], y1b[:, t * D:(t + 1) * D], 0.0, y1b[:, t * D:(t + 1) * D],
                        ALU.add, ALU.mult, accum_out=lnc[:, 20 + t:21 + t])
                assert ve[0] == VEs[("ln2", g)]
                vsync(vector.tensor_scalar_mul(lnc[:, 24:28], lnc[:, 16:20], 1.0 / D))
                vsync(vector.tensor_tensor(lnc[:, 28:32], lnc[:, 24:28], lnc[:, 24:28], ALU.mult))
                vsync(vector.scalar_tensor_tensor(lnc[:, 20:24], lnc[:, 20:24], 1.0 / D,
                                                  lnc[:, 28:32], ALU.mult, ALU.subtract))
                vinc(vector.tensor_scalar_add(lnc[:, 20:24], lnc[:, 20:24], EPS_LN))
                vector.wait_ge(s_gpc, GP_P2_0 + g * GP_G + GP_G)
                vinc(vector.scalar_tensor_tensor(lnc[:, 24:28], lnc[:, 24:28], -1.0,
                                                 lnc[:, 28:32], ALU.mult, ALU.mult))
            assert ve[0] == VE_END

        @block.scalar
        def _(scalar):
            sc = [0]
            def sinc(inst):
                inst.then_inc(s_sc, 1)
                sc[0] += 1
            for i in range(NT):
                scalar.wait_ge(s_mmp, i + 1)
                scalar.wait_ge(s_ve, i)
                sinc(scalar.activation(e_sb[:], P[0][:], AF.Exp, bias=0.0, scale=1.0))
                sinc(scalar.activation(r_sb[:], P[0][:], AF.Relu, bias=0.0, scale=1.0))
                scalar.wait_ge(s_mmk, i)
                sinc(scalar.activation(v_sb[:], P[1][:], AF.Copy, bias=0.0, scale=1.0))
            assert sc[0] == SC_P1
            scalar.wait_ge(s_ve, 33)
            scalar.dma_start(out=kv_in.rearrange("(p n) -> p n", p=128),
                             in_=kvs[:]).then_inc(s_kvd, 16)

            def emit_qer(gq):
                assert sc[0] == SCs[("qer", gq)]
                eb = e_sb if gq == 0 else e2
                rb = r_sb if gq == 0 else r2
                for co in range(8):
                    scalar.wait_ge(s_mm, MMs[("q", gq)] + co + 1)
                    if co >= 2:
                        scalar.wait_ge(s_ve, VEs[("comb", gq)] + co - 1)
                    half = (co % 2) * 512
                    src = P[2 + (co % 4) // 2][:, (co % 2) * 512:(co % 2) * 512 + 512]
                    sinc(scalar.activation(eb[:, half:half + 512], src, AF.Exp,
                                           bias=bq_sb[:, co:co + 1], scale=1.0))
                    sinc(scalar.activation(rb[:, half:half + 512], src, AF.Relu,
                                           bias=bq_sb[:, co:co + 1], scale=1.0))

            def emit_numden(gq):
                assert sc[0] == SCs[("nd", gq)]
                for p in range(NPAIR):
                    scalar.wait_ge(s_mm, MMs[("nd", gq)] + p + 1)
                    half = (p % 2) * 512
                    sinc(scalar.activation(xT0[:, p * 512:(p + 1) * 512],
                                           P[2][:, half:half + 512], AF.Copy,
                                           bias=0.0, scale=1.0))
                    sinc(scalar.activation(q_elu[:, p * 512:(p + 1) * 512],
                                           P[3][:, half:half + 512], AF.Copy,
                                           bias=0.0, scale=1.0))

            def emit_xf(gq):
                assert sc[0] == SCs[("xf", gq)]
                for t in range(4):
                    scalar.wait_ge(s_ve, VEs[("ln1", gq)] + 2)
                    sinc(scalar.activation(xf[:, t * D:(t + 1) * D],
                                           y1b[:, t * D:(t + 1) * D], AF.Identity,
                                           bias=lnc[:, 8 + t:9 + t],
                                           scale=lnc[:, 12 + t:13 + t]))

            def emit_silu(gq):
                assert sc[0] == SCs[("silu", gq)]
                for idx in range(NIO * 4):
                    c, t = idx // 4, idx % 4
                    io0, iow = IO_CHUNKS[c]
                    half = (idx % 2) * 512
                    pt = P[2 + idx % 2]
                    scalar.wait_ge(s_mm, MMs[("gu", gq)] + idx + 1)
                    if t >= 2:
                        scalar.wait_ge(s_ve, EXTi(gq, c, t - 2))
                    elif c >= 1:
                        scalar.wait_ge(s_ve, EXTi(gq, c - 1, t + 2))
                    sinc(scalar.activation(e_sb[:, half:half + iow], pt[:, 0:iow],
                                           AF.Silu, bias=0.0, scale=1.0))

            def emit_outs(gq):
                assert sc[0] == SCs[("outs", gq)]
                for t in range(4):
                    scalar.wait_ge(s_ve, VEs[("ln2", gq)] + 2)
                    i = gq * 4 + t
                    sinc(scalar.activation(y1b[:, t * D:(t + 1) * D],
                                           y1b[:, t * D:(t + 1) * D], AF.Identity,
                                           bias=lnc[:, 24 + t:25 + t],
                                           scale=lnc[:, 28 + t:29 + t]))
                    scalar.dma_start(out=out_d[i * 128:(i + 1) * 128, :],
                                     in_=y1b[:, t * D:(t + 1) * D]).then_inc(s_st, 16)

            emit_qer(0)
            emit_numden(0)
            emit_xf(0)
            for g in range(NG):
                if g + 1 < NG:
                    emit_qer(g + 1)
                    emit_numden(g + 1)
                emit_silu(g)
                for c in range(1, NC22, 2):
                    if c == 1:
                        if g == 0:
                            scalar.wait_ge(s_pt, 18)
                        else:
                            scalar.wait_ge(s_mm, MMs[("down", g - 1)] + 22)
                    else:
                        scalar.wait_ge(s_mm, MMs[("down", g)] + (c - 2) + 1)
                    scalar.dma_start(out=wdt[:, D:2 * D],
                                     in_=wdT_d[c * 128:(c + 1) * 128, :]
                                     ).then_inc(s_wd, 16)
                emit_outs(g)
                if g + 1 < NG:
                    emit_xf(g + 1)
            assert sc[0] == SC_END
            scalar.wait_ge(s_st, 16 * NT)

    return nc


_NC_CACHE = None


def kernel(L0_attn_x, L0_attn_Wq, L0_attn_bq, L0_attn_Wk, L0_attn_bk,
           L0_attn_Wv, L0_attn_bv, L0_attn_Wo, L0_attn_bo,
           L0_ffn_w_gate, L0_ffn_w_up, L0_ffn_w_down):
    global _NC_CACHE
    x = np.ascontiguousarray(np.asarray(L0_attn_x))
    B, T, C = x.shape
    assert (B, T, C) == (2, 16384, 1024)
    if _NC_CACHE is None:
        _NC_CACHE = build_nc()
    nc = _NC_CACHE
    in_maps = []
    for core in range(8):
        s, q = core // 4, core % 4
        shard = np.ascontiguousarray(x[s, q * T_CORE:(q + 1) * T_CORE, :])
        in_maps.append({
            "x": shard,
            "Wq": np.asarray(L0_attn_Wq), "Wk": np.asarray(L0_attn_Wk),
            "Wv": np.asarray(L0_attn_Wv), "Wo": np.asarray(L0_attn_Wo),
            "bq": np.asarray(L0_attn_bq), "bk": np.asarray(L0_attn_bk),
            "bv": np.asarray(L0_attn_bv), "bo": np.asarray(L0_attn_bo),
            "wg_sh": np.ascontiguousarray(np.asarray(L0_ffn_w_gate)[core * 352:(core + 1) * 352, :]),
            "wu_sh": np.ascontiguousarray(np.asarray(L0_ffn_w_up)[core * 352:(core + 1) * 352, :]),
            "wd_sh": np.ascontiguousarray(np.asarray(L0_ffn_w_down)[core * 128:(core + 1) * 128, :]),
        })
    res = run_bass_kernel_spmd(nc, in_maps, core_ids=list(range(8)))
    out = np.empty((2, 16384, 1024), dtype=np.float32)
    for core in range(8):
        s, q = core // 4, core % 4
        out[s, q * T_CORE:(q + 1) * T_CORE, :] = res.results[core]["out"]
    return out
